# version 01
# baseline (speedup 1.0000x reference)
import numpy as np
from contextlib import ExitStack
from functools import reduce

import concourse.bass as bass
import concourse.mybir as mybir
from concourse.bass_utils import run_bass_kernel_spmd

F32 = mybir.dt.float32
BF16 = mybir.dt.bfloat16
ALU = mybir.AluOpType
AF = mybir.ActivationFunctionType
AX = mybir.AxisListType

T = 2048
D = 1024
KT = 8
NH = 16
HD = 64
DCD = 4096
NE = 4096
RH = 4
EPS = 1e-5
NEG = -1e30


class Sem:
    def __init__(self, st, nc, name):
        self.h = st.enter_context(nc.semaphore(name))
        self.n = 0

    def inc(self, instr, k=1):
        instr.then_inc(self.h, k)
        self.n += k
        return instr


class Overlay:
    def __init__(self, nc, st):
        self.nc = nc
        self.st = st
        self.items = []
        self.size = 0

    def at(self, offset, name, shape, dtype):
        per_part = reduce(lambda a, b: a * b, shape[1:], 1) * mybir.dt.size(dtype)
        h = self.nc.virtual_sbuf_tensor(name, shape, dtype)
        self.items.append((h, offset))
        self.size = max(self.size, offset + per_part)
        return h

    def seal(self):
        sz = (self.size + 31) // 32 * 32
        self.base = self.st.enter_context(
            self.nc.sbuf_tensor("ovl_arena", [128, sz], mybir.dt.uint8))
        off0 = self.nc.lookup_mloc(self.base).addr
        for h, off in self.items:
            self.nc._tensor(h.name, h.shape, h.dtype, type="SB")
            mloc = self.nc.lookup_mloc(h)
            mloc.allocated = True
            mloc.addr = off0 + off


STOP = 99
SKIP_PV_ODD = False
SKIP_S_ODD = False
NORM_LVL = 4
SKIP_SIM = False


def build_nc():
    nc = bass.Bass("TRN2", target_bir_lowering=False)

    def din(name, shape):
        return nc.dram_tensor(name, list(shape), F32, kind="ExternalInput").ap()

    x_h = din("x", (T, D))
    lnw_h = din("lnw", (D,))
    lnb_h = din("lnb", (D,))
    wq_h = din("wq", (NH, D, HD))
    wk_h = din("wk", (NH, D, HD))
    wv_h = din("wv", (NH, D, HD))
    wo_h = din("wo", (D, D))
    wg_h = din("Wg", (D, DCD))
    bg_h = din("bg", (DCD,))
    wu_h = din("Wu", (D, DCD))
    bu_h = din("bu", (DCD,))
    wd_h = din("Wd", (DCD, D))
    bd_h = din("bd", (D,))
    wqr_h = din("Wq", (D, 512))
    keys_h = din("keys", (RH, 64, 2, 64))
    de_h = din("de", (NE, D))
    ue_h = din("ue", (NE, D))
    c_h = din("c", (D,))
    wmod_h = din("wmod", (D, 6 * D))

    out_h = nc.dram_tensor("out", [T, D], F32, kind="ExternalOutput").ap()
    dbg_h = nc.dram_tensor("dbg", [8, T], F32, kind="ExternalOutput").ap()
    xa_d = nc.dram_tensor("xa_d", [KT * 128, T], F32, kind="Internal").ap()
    x1_d = nc.dram_tensor("x1_d", [KT * 128, T], F32, kind="Internal").ap()
    m_d = nc.dram_tensor("m_d", [32 * 128, T], BF16, kind="Internal").ap()
    es_d = nc.dram_tensor("es_d", [KT * 128, T], BF16, kind="Internal").ap()
    ue_bf = nc.dram_tensor("ue_bf", [NE, D], BF16, kind="Internal").ap()
    wd_bf = nc.dram_tensor("wd_bf", [DCD, D], BF16, kind="Internal").ap()
    mod_d = nc.dram_tensor("mod_d", [6 * D], F32, kind="Internal").ap()

    st = ExitStack()
    with st:
        ovl = Overlay(nc, st)
        KB = 1024
        xT = ovl.at(0, "xT", [128, KT, T], F32)
        qst_e = ovl.at(0, "qst_e", [65, T], BF16)
        qst_o = ovl.at(4 * KB, "qst_o", [65, T], BF16)
        kst_e = ovl.at(8 * KB, "kst_e", [65, T], BF16)
        kst_o = ovl.at(12 * KB, "kst_o", [65, T], BF16)
        p_sb = ovl.at(16 * KB, "p_sb", [128, T], BF16)
        wqk = ovl.at(20 * KB, "wqk", [128, 2, KT, 256], BF16)
        wv_sb = ovl.at(28 * KB, "wv_sb", [128, KT, D], BF16)
        wo_sb = ovl.at(44 * KB, "wo_sb", [128, KT, D], BF16)
        qk_shift = ovl.at(60 * KB, "qk_shift", [128, T], BF16)
        det = ovl.at(0, "det", [128, KT, NE], BF16)
        A1 = 64 * KB
        h1b = ovl.at(A1, "h1b", [128, KT, T], BF16)
        onesF = ovl.at(A1, "onesF", [128, 128], F32)
        acat = ovl.at(A1, "acat", [128, KT, T], BF16)
        x1b = ovl.at(A1, "x1b", [128, KT, T], BF16)
        abuf = ovl.at(A1, "abuf", [128, NE], BF16)
        zbuf = ovl.at(A1 + 8 * KB, "zbuf", [128, NE], BF16)
        mbuf = ovl.at(A1 + 16 * KB, "mbuf", [128, NE], BF16)
        vbuf = ovl.at(A1 + 24 * KB, "vbuf", [128, NE], BF16)
        mcache = ovl.at(A1, "mcache", [128, 32, 512], BF16)
        A2 = 96 * KB
        xb = ovl.at(A2, "xb", [128, KT, T], BF16)
        y_sb = ovl.at(A2, "y_sb", [128, KT, T], BF16)
        h2 = ovl.at(A2, "h2", [128, KT, T], BF16)
        A3 = 128 * KB
        v_sb = ovl.at(A3, "v_sb", [128, 16, 1040], BF16)
        vt = ovl.at(A3, "vt", [128, 32, 512], BF16)
        A4 = 160 * KB + 512
        wmr = ovl.at(A4, "wmr", [128, 2, KT, 256], BF16)
        wgr = ovl.at(A4, "wgr", [128, 2, KT, 128], BF16)
        wur = ovl.at(A4 + 4 * KB, "wur", [128, 2, KT, 128], BF16)
        wdt = ovl.at(A4, "wdt", [128, 32, 128], BF16)
        der = ovl.at(A4, "der", [128, 2, D], BF16)
        uer = ovl.at(A4 + 4 * KB, "uer", [128, 2, D], BF16)
        A5 = A4 + 8 * KB
        qrt = ovl.at(A5, "qrt", [128, 4, 128], BF16)
        sq64 = ovl.at(A5 + 1 * KB, "sq64", [64, T], BF16)
        wqr_sb = ovl.at(A5 + 1 * KB, "wqr_sb", [128, KT, 512], BF16)
        mring = ovl.at(A5 + 1 * KB, "mring", [128, 2, T], BF16)
        xin = ovl.at(A3, "xin", [128, 1, D], F32)
        keysR = ovl.at(A3 + 28 * KB, "keysR", [128, 8, 64], BF16)
        sim_f = ovl.at(A4, "sim_f", [128, 512], F32)
        uvin = ovl.at(A4 + 2 * KB, "uvin", [128, 512], F32)
        A6 = A5 + 9 * KB
        x1st = ovl.at(A6, "x1st", [128, 512], F32)
        qrt2 = ovl.at(A6, "qrt2", [64, 4, 128], BF16)
        gstage = ovl.at(A6, "gstage", [128, 1024], BF16)
        ostage = ovl.at(A6 + 2 * KB, "ostage", [128, 512], F32)
        u_p = ovl.at(A6 + 2 * KB, "u_p", [128, RH, 64], BF16)
        v_p = ovl.at(A6 + 2 * KB + 512, "v_p", [128, RH, 64], BF16)
        r_f = ovl.at(A6 + 3 * KB, "r_f", [128, RH, 64], F32)
        grid = ovl.at(A6 + 4 * KB, "grid", [128, 96], F32)
        g2 = ovl.at(A6 + 4 * KB + 384, "g2", [128, 96], F32)
        scr64 = ovl.at(A6 + 4 * KB + 768, "scr64", [128, 64], F32)
        esr = ovl.at(A6 + 5 * KB + 64, "esr", [128, 2, 512], BF16)
        ovl.seal()

        def sbt(name, shape, dtype):
            return st.enter_context(nc.sbuf_tensor(name, shape, dtype))

        identF = sbt("identF", [128, 128], F32)
        identB = sbt("identB", [128, 128], BF16)
        onesb = sbt("onesb", [128, 1], BF16)
        onesRF = sbt("onesRF", [1, 128], F32)
        onesRB = sbt("onesRB", [1, 128], BF16)
        rowA = sbt("rowA", [1, T], F32)
        a_row = sbt("a_row", [1, T], BF16)
        kmax_t = sbt("kmax_t", [1, 1], F32)
        csil = sbt("csil", [128, 8], F32)
        csilb = sbt("csilb", [128, 8], BF16)
        tmpf = sbt("tmpf", [128, 512], F32)
        keysT = sbt("keysT", [128, 8, 64], BF16)
        m8g = sbt("m8g", [128, 8], F32)
        t8 = sbt("t8", [128, 8], F32)
        t8y = sbt("t8y", [128, 8], F32)
        den8 = sbt("den8", [128, RH * 8], F32)
        den8e = sbt("den8e", [128, RH * 8], F32)

        def cols8(tag):
            return [sbt(f"c_{tag}_{i}", [128, 1], F32) for i in range(8)]

        sa_t = cols8("sa")
        ca_t = cols8("ca")
        ga_t = cols8("ga")
        sbv_t = cols8("sb")
        cb_t = cols8("cb")
        gb_t = cols8("gb")
        lnw_t = cols8("lnw")
        lnb_t = cols8("lnb")
        bd_t = cols8("bd")
        bg_t = [sbt(f"bg_{i}", [128, 1], F32) for i in range(32)]
        bu_t = [sbt(f"bu_{i}", [128, 1], F32) for i in range(32)]
        c1_t = sbt("c1_t", [128, 1], F32)
        c2_t = sbt("c2_t", [128, 1], F32)
        c12_t = sbt("c12_t", [128, 1], F32)
        s9x_t = sbt("s9x_t", [128, 1], F32)
        s9y_t = sbt("s9y_t", [128, 1], F32)
        th8_t = sbt("th8_t", [128, 1], F32)
        th9_t = sbt("th9_t", [128, 1], F32)
        thm_t = sbt("thm_t", [128, 1], F32)
        kap_t = sbt("kap_t", [128, 1], F32)

        psA = [st.enter_context(nc.psum_tensor(f"psA{i}", [128, 512], F32))
               for i in range(3)]
        psPV = st.enter_context(nc.psum_tensor("psPV", [128, 2048], F32))
        psB = st.enter_context(nc.psum_tensor("psB", [128, 1024], BF16))

        dS = Sem(st, nc, "dS")
        sGB = Sem(st, nc, "sGB")
        sWM = [Sem(st, nc, "sWM0"), Sem(st, nc, "sWM1")]
        sWVO = Sem(st, nc, "sWVO")
        sQK = [Sem(st, nc, "sQK0"), Sem(st, nc, "sQK1")]
        sSH = Sem(st, nc, "sSH")
        sMR = Sem(st, nc, "sMR")
        sKW = Sem(st, nc, "sKW")
        sDE = [Sem(st, nc, "sDE0"), Sem(st, nc, "sDE1")]
        sWG = [Sem(st, nc, "sWG0"), Sem(st, nc, "sWG1")]
        sWD = Sem(st, nc, "sWD")
        sBI = Sem(st, nc, "sBI")
        sXIN = Sem(st, nc, "sXIN")
        sXA = Sem(st, nc, "sXA")
        sMOD = Sem(st, nc, "sMOD")
        sX1 = Sem(st, nc, "sX1")
        sMC = Sem(st, nc, "sMC")
        sUE = [Sem(st, nc, "sUE0"), Sem(st, nc, "sUE1")]
        sES = Sem(st, nc, "sES")
        sQ2 = Sem(st, nc, "sQ2")
        dG = Sem(st, nc, "dG")
        dO = Sem(st, nc, "dO")
        eT = Sem(st, nc, "eT")
        eV = Sem(st, nc, "eV")
        eS = Sem(st, nc, "eS")
        eG = Sem(st, nc, "eG")

        block = st.enter_context(nc.Block())
        guard = {}

        def pguard(t, eng):
            g = guard.get(id(t))
            if g:
                eng.wait_ge(g[0].h, g[1])

        def setg(t, sem):
            guard[id(t)] = (sem, sem.n)

        def sec(name, fn):
            getattr(block, name)(fn)

        rot = {"i": 0}

        def next_psA():
            rot["i"] = (rot["i"] + 1) % 3
            return psA[rot["i"]]

        def ph0_v(v):
            v.memset(onesF[:], 1.0)
            v.memset(onesb[:], 1.0)
            v.memset(onesRF[:], 1.0)
            v.memset(onesRB[:], 1.0)
            v.drain()
            eV.inc(v.nop())
        sec("vector", ph0_v)

        def ph0_g(g):
            g.wait_ge(eV.h, eV.n)
            eG.inc(g.affine_select(identF[:], onesF[:], pattern=[[1, 128]],
                                   compare_op=ALU.is_equal, fill=0.0,
                                   channel_multiplier=-1, base=0))
            sGB.inc(g.dma_start(out=ue_bf, in_=ue_h), 16)
            sGB.inc(g.dma_start(out=wd_bf, in_=wd_h), 16)
        sec("gpsimd", ph0_g)

        def ph0_v2(v):
            v.wait_ge(eG.h, eG.n)
            v.tensor_copy(identB[:], identF[:])
            v.drain()
            eV.inc(v.nop())
        sec("vector", ph0_v2)

        def ph0_s(s):
            for i in range(8):
                sBI.inc(s.dma_start(out=lnw_t[i][:], in_=lnw_h[i * 128:(i + 1) * 128].unsqueeze(1)), 16)
                sBI.inc(s.dma_start(out=lnb_t[i][:], in_=lnb_h[i * 128:(i + 1) * 128].unsqueeze(1)), 16)
                sBI.inc(s.dma_start(out=bd_t[i][:], in_=bd_h[i * 128:(i + 1) * 128].unsqueeze(1)), 16)
            for i in range(32):
                sBI.inc(s.dma_start(out=bg_t[i][:], in_=bg_h[i * 128:(i + 1) * 128].unsqueeze(1)), 16)
                sBI.inc(s.dma_start(out=bu_t[i][:], in_=bu_h[i * 128:(i + 1) * 128].unsqueeze(1)), 16)
            sBI.inc(s.dma_start(out=csil[:], in_=c_h.rearrange("(kt p) -> p kt", p=128), allow_slow_non_contiguous=True), 16)
        sec("sync", ph0_s)
        setup_dS = sBI.n

        xin_free = [None, None]
        for tt in range(16):
            sl = tt % 2

            def s_ld(s, tt=tt, sl=sl):
                if xin_free[0] is not None:
                    s.wait_ge(eT.h, xin_free[0])
                sXIN.inc(s.dma_start(out=xin[:, 0, :], in_=x_h[tt * 128:(tt + 1) * 128, :]), 16)
            sec("sync", s_ld)
            need = sXIN.n
            for kt in range(KT):
                ps = next_psA()

                def t_fn(t, tt=tt, kt=kt, need=need, ps=ps):
                    if kt == 0:
                        t.wait_ge(sXIN.h, need)
                    if tt == 0 and kt == 0:
                        t.wait_ge(eG.h, 1)
                    pguard(ps, t)
                    eT.inc(t.transpose(ps[:, 0:128], xin[:, 0, kt * 128:(kt + 1) * 128], identF[:]))
                sec("tensor", t_fn)
                tick = eT.n

                def s2_fn(s, tt=tt, kt=kt, ps=ps, tick=tick):
                    s.wait_ge(eT.h, tick)
                    eS.inc(s.copy(xT[:, kt, tt * 128:(tt + 1) * 128], ps[:, 0:128]))
                    setg(ps, eS)
                sec("scalar", s2_fn)
            xin_free[0] = eT.n

        if STOP <= 1:
            def _fin5(s):
                s.wait_ge(dS.h, dS.n)
                s.wait_ge(dG.h, dG.n)
                dS.inc(s.dma_start(out=rowA[0:1, :], in_=x1_d[0:1, :]), 16)
                s.wait_ge(dS.h, dS.n)
                dO.inc(s.dma_start(out=dbg_h[1:2, :], in_=rowA[0:1, :]), 16)
                dO.inc(s.dma_start(out=dbg_h[5:6, 0:512], in_=x1st[0:1, :]), 16)
                s.wait_ge(dO.h, dO.n)

            def _fin5g(g):
                g.wait_ge(dO.h, dO.n)
                dO.inc(g.dma_start(out=dbg_h[0:1, 0:128], in_=lnw_t[0][:], allow_slow_non_contiguous=True), 16)
                dO.inc(g.dma_start(out=dbg_h[2:3, :], in_=y_sb[0:1, 0, :]), 16)
                dO.inc(g.dma_start(out=dbg_h[3:4, :], in_=acat[0:1, 0, :]), 16)
                dO.inc(g.dma_start(out=dbg_h[4:5, :], in_=qst_e[0:1, :]), 16)
                dO.inc(g.dma_start(out=dbg_h[6:7, 0:1040], in_=v_sb[0:1, 0, :]), 16)
                dO.inc(g.dma_start(out=dbg_h[7:8, 0:128], in_=xin[0:1, 0, 0:128]), 16)
                g.wait_ge(dO.h, dO.n)
            sec("sync", _fin5)
            sec("gpsimd", _fin5g)
            return nc

        def ph2_s(s):
            s.wait_ge(sBI.h, setup_dS)
            eS.inc(s.activation(csil[:], csil[:], AF.Silu))
        sec("scalar", ph2_s)

        def ph2_v(v):
            v.wait_ge(eS.h, eS.n)
            v.drain()
            eV.inc(v.tensor_copy(csilb[:], csil[:]))
        sec("vector", ph2_v)
        csil_tick = eV.n

        wm_free = [None, None]
        for cc in range(24):
            sl = cc % 2

            def g_fn(g, cc=cc, sl=sl):
                if wm_free[sl] is not None:
                    g.wait_ge(eT.h, wm_free[sl])
                for kt in range(KT):
                    src = wmod_h[kt * 128:(kt + 1) * 128, cc * 256:(cc + 1) * 256]
                    sWM[sl].inc(g.dma_start(out=wmr[:, sl, kt, :], in_=src), 16)
            sec("gpsimd", g_fn)
            need = sWM[sl].n
            ps = next_psA()

            def t_fn(t, cc=cc, sl=sl, need=need, ps=ps):
                t.wait_ge(sWM[sl].h, need)
                if cc == 0:
                    t.wait_ge(eV.h, csil_tick)
                pguard(ps, t)
                for kt in range(KT):
                    mm = t.matmul(ps[0:1, 0:256], csilb[:, kt:kt + 1], wmr[:, sl, kt, :],
                                  start=(kt == 0), stop=(kt == KT - 1),
                                  skip_group_check=True)
                eT.inc(mm)
            sec("tensor", t_fn)
            wm_free[sl] = eT.n
            mm_tick = eT.n

            def s_fn(s, ps=ps, mm_tick=mm_tick):
                s.wait_ge(eT.h, mm_tick)
                s.wait_ge(sMOD.h, sMOD.n)
                eS.inc(s.copy(rowA[0:1, 0:256], ps[0:1, 0:256]))
                setg(ps, eS)
            sec("scalar", s_fn)

            def s_st(sy, cc=cc, tick=eS.n):
                sy.wait_ge(eS.h, tick)
                sMOD.inc(sy.dma_start(out=mod_d[cc * 256:(cc + 1) * 256].unsqueeze(0),
                                       in_=rowA[0:1, 0:256]), 16)
            sec("sync", s_st)

        def ph2_reload(s):
            s.wait_ge(sMOD.h, sMOD.n)
            tiles = [sa_t, ca_t, ga_t, sbv_t, cb_t, gb_t]
            for gset in range(6):
                for i in range(8):
                    off = gset * 1024 + i * 128
                    sMOD.inc(s.dma_start(out=tiles[gset][i][:], in_=mod_d[off:off + 128].unsqueeze(1)), 16)
        sec("sync", ph2_reload)

        def ph2_clamp(v):
            v.wait_ge(sMOD.h, sMOD.n)
            v.wait_ge(sBI.h, sBI.n)
            for i in range(8):
                v.tensor_scalar(ca_t[i][:], ca_t[i][:], 1.0, None, ALU.add)
                v.tensor_scalar(cb_t[i][:], cb_t[i][:], 1.0, None, ALU.add)
            v.drain()
            for tset in (ga_t, gb_t):
                for i in range(8):
                    g = tset[i]
                    v.tensor_scalar(c1_t[:], g[:], 0.0, None, ALU.is_ge)
                    v.drain()
                    v.tensor_scalar(c1_t[:], c1_t[:], 2.0, -1.0, ALU.mult, ALU.add)
                    v.tensor_scalar(c2_t[:], g[:], -1.0, None, ALU.mult)
                    v.drain()
                    v.tensor_tensor(c2_t[:], c2_t[:], g[:], ALU.max)
                    v.drain()
                    v.tensor_scalar(c2_t[:], c2_t[:], 0.25, None, ALU.max)
                    v.drain()
                    v.tensor_tensor(g[:], c1_t[:], c2_t[:], ALU.mult)
                    v.drain()
            v.drain()
            eV.inc(v.nop())
        sec("vector", ph2_clamp)

        if STOP <= 2:
            def _fin(s):
                s.wait_ge(dS.h, dS.n)
                s.wait_ge(dG.h, dG.n)
                dS.inc(s.dma_start(out=rowA[0:1, :], in_=mod_d[0:T].unsqueeze(0)), 16)
                s.wait_ge(dS.h, dS.n)
                dO.inc(s.dma_start(out=dbg_h[4:5, :], in_=rowA[0:1, :]), 16)
                dO.inc(s.dma_start(out=dbg_h[6:7, 0:128], in_=lnw_t[0][:], allow_slow_non_contiguous=True), 16)
                dO.inc(s.dma_start(out=dbg_h[7:8, 0:128], in_=ca_t[0][:], allow_slow_non_contiguous=True), 16)
                s.wait_ge(dO.h, dO.n)
            sec("sync", _fin)
            return nc

        def emit_ln_stats(val, sq):
            first = [True]
            for which, srcv in ((0, val), (1, sq)):
                for n4 in range(4):
                    ps = next_psA()

                    def t_fn(t, which=which, srcv=srcv, n4=n4, ps=ps):
                        if first[0]:
                            t.wait_ge(eS.h, eS.n)
                            t.wait_ge(eV.h, eV.n)
                            first[0] = False
                        pguard(ps, t)
                        for kt in range(KT):
                            mm = t.matmul(ps[0:1, 0:512], onesb[:],
                                          srcv[:, kt, n4 * 512:(n4 + 1) * 512],
                                          start=(kt == 0), stop=(kt == KT - 1),
                                          skip_group_check=True)
                        eT.inc(mm)
                    sec("tensor", t_fn)
                    tick = eT.n

                    def s_fn(s, which=which, n4=n4, ps=ps, tick=tick):
                        if which == 0 and n4 == 0:
                            s.wait_ge(sMR.h, sMR.n)
                            s.wait_ge(sMOD.h, sMOD.n)
                        s.wait_ge(eT.h, tick)
                        dst = a_row[0:1, n4 * 512:n4 * 512 + 512] if which == 0 else rowA[0:1, n4 * 512:n4 * 512 + 512]
                        eS.inc(s.copy(dst, ps[0:1, 0:512]))
                        setg(ps, eS)
                    sec("scalar", s_fn)

            def v_fn(v):
                v.wait_ge(eS.h, eS.n)
                sm = a_row[0:1, 0:T]
                sq_ = rowA[0:1, 0:T]
                v.tensor_scalar(sm, sm, 1.0 / D, None, ALU.mult)
                v.tensor_scalar(sq_, sq_, 1.0 / D, None, ALU.mult)
                v.drain()
                for _c in range(4):
                    v.tensor_tensor(tmpf[0:1, 0:512], sm[0:1, _c * 512:(_c + 1) * 512],
                                    sm[0:1, _c * 512:(_c + 1) * 512], ALU.mult)
                    v.drain()
                    v.tensor_tensor(rowA[0:1, _c * 512:(_c + 1) * 512],
                                    rowA[0:1, _c * 512:(_c + 1) * 512], tmpf[0:1, 0:512], ALU.subtract)
                    v.drain()
                v.tensor_scalar(rowA[0:1, 0:T], rowA[0:1, 0:T], EPS, None, ALU.add)
                v.drain()
                eV.inc(v.nop())
            sec("vector", v_fn)

            def s2_fn(s):
                s.wait_ge(eV.h, eV.n)
                s.wait_ge(eS.h, eS.n)
                eS.inc(s.activation(rowA[0:1, 0:T], rowA[0:1, 0:T], AF.Sqrt))
            sec("scalar", s2_fn)

            def v2_fn(v):
                v.wait_ge(eS.h, eS.n)
                v.reciprocal(rowA[0:1, 0:T], rowA[0:1, 0:T])
                v.drain()
                v.scalar_tensor_tensor(a_row[0:1, 0:T], a_row[0:1, 0:T], -1.0,
                                       rowA[0:1, 0:T], ALU.mult, ALU.mult)
                v.drain()
                eV.inc(v.nop())
            sec("vector", v2_fn)

        def emit_bcast(n4):
            psa, psb_ = next_psA(), next_psA()

            def t_fn(t, psa=psa, psb_=psb_, n4=n4):
                t.wait_ge(eV.h, eV.n)
                pguard(psa, t)
                pguard(psb_, t)
                t.matmul(psa[:, 0:512], onesRF[0:1, :], rowA[0:1, n4 * 512:(n4 + 1) * 512],
                         start=True, stop=True, skip_group_check=True)
                eT.inc(t.matmul(psb_[:, 0:512], onesRB[0:1, :], a_row[0:1, n4 * 512:(n4 + 1) * 512],
                                start=True, stop=True, skip_group_check=True))
            sec("tensor", t_fn)
            return psa, psb_, eT.n

        def ph3_sq(s):
            for kt in range(KT):
                eS.inc(s.activation(xb[:, kt, :], xT[:, kt, :], AF.Square))
        sec("scalar", ph3_sq)

        def ph3_cp(v):
            v.wait_ge(eS.h, eS.n)
            for kt in range(KT):
                eV.inc(v.tensor_copy(h1b[:, kt, :], xT[:, kt, :]))
        sec("vector", ph3_cp)

        emit_ln_stats(h1b, xb)
        for n4 in range(4):
            psa, psb_, tick = emit_bcast(n4)

            def v_fn(v, psa=psa, psb_=psb_, tick=tick, n4=n4):
                v.wait_ge(eT.h, tick)
                for kt in range(KT):
                    sl = (slice(None), kt, slice(n4 * 512, (n4 + 1) * 512))
                    v.tensor_tensor(tmpf[:], xT[sl], psa[:, 0:512], ALU.mult)
                    v.drain()
                    v.tensor_tensor(tmpf[:], tmpf[:], psb_[:, 0:512], ALU.add)
                    v.drain()
                    v.tensor_scalar(h1b[sl], tmpf[:], ca_t[kt][:], sa_t[kt][:], ALU.mult, ALU.add)
                    v.drain()
                    last = v.scalar_tensor_tensor(xT[sl], h1b[sl], ga_t[kt][:], xT[sl], ALU.mult, ALU.add)
                    v.drain()
                v.drain()
                eV.inc(last)
                setg(psa, eV)
                setg(psb_, eV)
            sec("vector", v_fn)

        def ph3_spill(s):
            s.wait_ge(eV.h, eV.n)
            for kt in range(KT):
                sXA.inc(s.dma_start(out=xa_d[kt * 128:(kt + 1) * 128, :], in_=xT[:, kt, :]), 16)
        sec("sync", ph3_spill)
        xa_spill = sXA.n

        if STOP <= 3:
            def _fin3(s):
                s.wait_ge(dS.h, dS.n)
                s.wait_ge(dG.h, dG.n)
                dS.inc(s.dma_start(out=rowA[0:1, :], in_=xa_d[0:1, :]), 16)
                s.wait_ge(dS.h, dS.n)
                dO.inc(s.dma_start(out=dbg_h[0:1, :], in_=rowA[0:1, :]), 16)
                s.wait_ge(dO.h, dO.n)
            sec("sync", _fin3)
            return nc

        def ph4_sq(s):
            s.wait_ge(eV.h, eV.n)
            for kt in range(KT):
                eS.inc(s.activation(xb[:, kt, :], h1b[:, kt, :], AF.Square))
        sec("scalar", ph4_sq)
        emit_ln_stats(h1b, xb)
        for n4 in range(4):
            psa, psb_, tick = emit_bcast(n4)

            def v_fn(v, psa=psa, psb_=psb_, tick=tick, n4=n4):
                v.wait_ge(eT.h, tick)
                for kt in range(KT):
                    sl = (slice(None), kt, slice(n4 * 512, (n4 + 1) * 512))
                    v.tensor_tensor(tmpf[:], h1b[sl], psa[:, 0:512], ALU.mult)
                    v.drain()
                    v.tensor_tensor(tmpf[:], tmpf[:], psb_[:, 0:512], ALU.add)
                    v.drain()
                    last = v.tensor_scalar(y_sb[sl], tmpf[:], lnw_t[kt][:], lnb_t[kt][:], ALU.mult, ALU.add)
                    v.drain()
                v.drain()
                eV.inc(last)
                setg(psa, eV)
                setg(psb_, eV)
            sec("vector", v_fn)
        y_tick = eV.n

        if STOP <= 4:
            def _fin5(s):
                s.wait_ge(dS.h, dS.n)
                s.wait_ge(dG.h, dG.n)
                dS.inc(s.dma_start(out=rowA[0:1, :], in_=x1_d[0:1, :]), 16)
                s.wait_ge(dS.h, dS.n)
                dO.inc(s.dma_start(out=dbg_h[1:2, :], in_=rowA[0:1, :]), 16)
                dO.inc(s.dma_start(out=dbg_h[5:6, 0:512], in_=x1st[0:1, :]), 16)
                s.wait_ge(dO.h, dO.n)

            def _fin5g(g):
                g.wait_ge(dO.h, dO.n)
                dO.inc(g.dma_start(out=dbg_h[0:1, 0:128], in_=lnw_t[0][:], allow_slow_non_contiguous=True), 16)
                dO.inc(g.dma_start(out=dbg_h[2:3, :], in_=y_sb[0:1, 0, :]), 16)
                dO.inc(g.dma_start(out=dbg_h[3:4, :], in_=acat[0:1, 0, :]), 16)
                dO.inc(g.dma_start(out=dbg_h[4:5, :], in_=qst_e[0:1, :]), 16)
                dO.inc(g.dma_start(out=dbg_h[6:7, 0:1040], in_=v_sb[0:1, 0, :]), 16)
                dO.inc(g.dma_start(out=dbg_h[7:8, 0:128], in_=xin[0:1, 0, 0:128]), 16)
                g.wait_ge(dO.h, dO.n)
            sec("sync", _fin5)
            sec("gpsimd", _fin5g)
            return nc

        def ph5_g(g):
            g.wait_ge(sXA.h, xa_spill)
            g.memset(kst_e[64:65, :], -1.0)
            eG.inc(g.memset(kst_o[64:65, :], -1.0))
            for kt in range(KT):
                src = wv_h[:, kt * 128:(kt + 1) * 128, :].transpose([1, 0, 2])
                sWVO.inc(g.dma_start(out=wv_sb[:, kt, :].rearrange("p (h e) -> p h e", e=64), in_=src), 16)
            for kt in range(KT):
                sWVO.inc(g.dma_start(out=wo_sb[:, kt, :], in_=wo_h[kt * 128:(kt + 1) * 128, :]), 16)
        sec("gpsimd", ph5_g)
        wv_tick = sWVO.n

        for tt in range(16):
            for half in range(2):
                ps = next_psA()

                def t_fn(t, tt=tt, half=half, ps=ps):
                    if tt == 0 and half == 0:
                        t.wait_ge(sWVO.h, wv_tick)
                        t.wait_ge(eV.h, y_tick)
                        t.wait_ge(sXA.h, xa_spill)
                    pguard(ps, t)
                    for kt in range(KT):
                        mm = t.matmul(ps[:, 0:512], y_sb[:, kt, tt * 128:(tt + 1) * 128],
                                      wv_sb[:, kt, half * 512:(half + 1) * 512],
                                      start=(kt == 0), stop=(kt == KT - 1),
                                      skip_group_check=True)
                    eT.inc(mm)
                sec("tensor", t_fn)
                tick = eT.n

                def v_fn(v, tt=tt, half=half, ps=ps, tick=tick):
                    v.wait_ge(eT.h, tick)
                    dst = v_sb[:, tt, half * 520:(half + 1) * 520].rearrange("p (h e) -> p h e", e=65)[:, :, 0:64]
                    eV.inc(v.tensor_copy(dst, ps[:, 0:512].rearrange("p (h e) -> p h e", e=64)))
                    setg(ps, eV)
                sec("vector", v_fn)

        def ph5_ones(g):
            g.wait_ge(eV.h, eV.n)
            for tt in range(16):
                eG.inc(g.memset(v_sb[:, tt, :].rearrange("p (h e) -> p h e", e=65)[:, :, 64:65], 1.0))
        sec("gpsimd", ph5_ones)

        if STOP <= 41:
            def _fin41(s):
                s.wait_ge(dS.h, dS.n)
                s.wait_ge(dG.h, dG.n)
            sec("sync", _fin41)
            return nc

        wqk_free = [None, None]
        for pi in range(8):
            slp = pi % 2

            def g_ld(g, pi=pi, slp=slp):
                if wqk_free[slp] is not None:
                    g.wait_ge(eT.h, wqk_free[slp])
                for kt in range(KT):
                    srcq = wq_h[2 * pi:2 * pi + 2, kt * 128:(kt + 1) * 128, :].transpose([1, 0, 2])
                    srck = wk_h[2 * pi:2 * pi + 2, kt * 128:(kt + 1) * 128, :].transpose([1, 0, 2])
                    sQK[slp].inc(g.dma_start(out=wqk[:, slp, kt, 0:128].rearrange("p (h e) -> p h e", e=64), in_=srcq), 16)
                    sQK[slp].inc(g.dma_start(out=wqk[:, slp, kt, 128:256].rearrange("p (h e) -> p h e", e=64), in_=srck), 16)
            sec("gpsimd", g_ld)
            wqk_tick = sQK[slp].n

            for which in ("q", "k"):
                base = 0 if which == "q" else 128
                scale = 0.125 if which == "q" else 1.0
                st_e = qst_e if which == "q" else kst_e
                st_o = qst_o if which == "q" else kst_o
                for n4 in range(4):
                    ps = next_psA()

                    def t_fn(t, base=base, wqk_tick=wqk_tick, slp=slp, n4=n4, ps=ps):
                        if n4 == 0:
                            t.wait_ge(sQK[slp].h, wqk_tick)
                        pguard(ps, t)
                        for kt in range(KT):
                            mm = t.matmul(ps[:, 0:512], wqk[:, slp, kt, base:base + 128],
                                          y_sb[:, kt, n4 * 512:(n4 + 1) * 512],
                                          start=(kt == 0), stop=(kt == KT - 1),
                                          skip_group_check=True)
                        eT.inc(mm)
                    sec("tensor", t_fn)
                    tick = eT.n

                    def s_fn(s, st_e=st_e, scale=scale, n4=n4, ps=ps, tick=tick):
                        if n4 == 0:
                            s.wait_ge(sSH.h, sSH.n)
                        s.wait_ge(eT.h, tick)
                        s.mul(st_e[0:64, n4 * 512:(n4 + 1) * 512], ps[0:64, 0:512], scale)
                        eS.inc(s.mul(qk_shift[64:128, n4 * 512:(n4 + 1) * 512], ps[64:128, 0:512], scale))
                        setg(ps, eS)
                    sec("scalar", s_fn)
                shift_need = eS.n

                def g_sh(g, st_o=st_o, shift_need=shift_need):
                    g.wait_ge(eS.h, shift_need)
                    sSH.inc(g.dma_start(out=st_o[0:64, :], in_=qk_shift[64:128, :]), 16)
                sec("gpsimd", g_sh)
            pair_g = sSH.n
            pair_s = eS.n
            if STOP <= 42:
                def _fin42(s):
                    s.wait_ge(dS.h, dS.n)
                    s.wait_ge(dG.h, dG.n)
                sec("sync", _fin42)
                return nc


            for hh in range(2):
                h = 2 * pi + hh
                qst = qst_e if hh == 0 else qst_o
                kst = kst_e if hh == 0 else kst_o

                LVL = NORM_LVL if (h % 2 == 1) else 4

                def s_n1(s, qst=qst, pair_g=pair_g, pair_s=pair_s):
                    s.wait_ge(sSH.h, pair_g)
                    s.wait_ge(eS.h, pair_s)
                    if LVL == 1 or LVL >= 4:
                        eS.inc(s.activation(sq64[0:64, :], qst[0:64, :], AF.Square))

                def v_n1(v, qst=qst, pair_g=pair_g):
                    if LVL in (11, 12, 13, 14):
                        v.wait_ge(dG.h, pair_g)
                        v.wait_ge(eS.h, pair_s)
                        srcq = qst_e if LVL == 12 else qst
                        dstq = p_sb if LVL == 13 else sq64
                        if LVL == 14:
                            v.tensor_copy(p_sb[0:64, 0:512], qst[0:64, 0:512])
                        else:
                            v.tensor_tensor(dstq[0:64, :], srcq[0:64, :], srcq[0:64, :], ALU.mult)
                        v.drain()
                        eV.inc(v.nop())
                sec("scalar", s_n1)
                sec("vector", v_n1)
                for n4 in range(4):
                    ps = next_psA()

                    def t_n1(t, n4=n4, ps=ps):
                        if n4 == 0:
                            t.wait_ge(eS.h, eS.n)
                        pguard(ps, t)
                        eT.inc(t.matmul(ps[0:1, 0:512], onesb[0:64, :],
                                        sq64[0:64, n4 * 512:(n4 + 1) * 512],
                                        start=True, stop=True, skip_group_check=True))
                    sec("tensor", t_n1)
                    tick = eT.n

                    def s_n2a(s, n4=n4, ps=ps, tick=tick):
                        if n4 == 0:
                            s.wait_ge(sMR.h, sMR.n)
                        s.wait_ge(eT.h, tick)
                        eS.inc(s.copy(rowA[0:1, n4 * 512:n4 * 512 + 512], ps[0:1, 0:512]))
                        setg(ps, eS)
                    sec("scalar", s_n2a)

                def s_n2(s, kst=kst):
                    s.wait_ge(eS.h, eS.n)
                    s.activation(rowA[0:1, 0:T], rowA[0:1, 0:T], AF.Sqrt)
                    eS.inc(s.activation(sq64[0:64, :], kst[0:64, :], AF.Square))
                sec("scalar", s_n2)
                for n4 in range(4):
                    ps = next_psA()

                    def t_n2(t, n4=n4, ps=ps):
                        if n4 == 0:
                            t.wait_ge(eS.h, eS.n)
                        pguard(ps, t)
                        eT.inc(t.matmul(ps[0:1, 0:512], onesb[0:64, :],
                                        sq64[0:64, n4 * 512:(n4 + 1) * 512],
                                        start=True, stop=True, skip_group_check=True))
                    sec("tensor", t_n2)
                    tick = eT.n

                    def s_n3a(s, n4=n4, ps=ps, tick=tick):
                        s.wait_ge(eT.h, tick)
                        eS.inc(s.copy(a_row[0:1, n4 * 512:n4 * 512 + 512], ps[0:1, 0:512]))
                        setg(ps, eS)
                    sec("scalar", s_n3a)

                def s_n3(s):
                    s.wait_ge(eS.h, eS.n)
                    eS.inc(s.activation(a_row[0:1, 0:T], a_row[0:1, 0:T], AF.Sqrt))
                sec("scalar", s_n3)

                def v_n(v):
                    v.wait_ge(eS.h, eS.n)
                    if LVL < 3:
                        v.drain()
                        eV.inc(v.nop())
                        return
                    v.tensor_reduce(kmax_t[:], a_row[0:1, 0:T], AX.X, ALU.max)
                    v.drain()
                    v.tensor_tensor(rowA[0:1, 0:T], rowA[0:1, 0:T],
                                    kmax_t[0:1, 0:1].broadcast_to([1, T]), ALU.mult)
                    v.drain()
                    eV.inc(v.nop())
                sec("vector", v_n)

                def g_m(g, qst=qst):
                    g.wait_ge(eV.h, eV.n)
                    if LVL >= 4:
                        sMR.inc(g.dma_start(out=qst[64:65, :], in_=rowA[0:1, 0:T]), 16)
                sec("gpsimd", g_m)
                head_g = sMR.n

                first_s = True
                rrange = [] if (SKIP_S_ODD and h % 2 == 1) else list(range(16))
                for r in rrange:
                    chunks = []
                    n0 = 128 * r
                    for c in range(r // 4, 4):
                        a = max(n0, 512 * c)
                        b = 512 * (c + 1)
                        chunks.append((a, b, c))
                    exp_ticks = []
                    for ci, (a, b, c) in enumerate(chunks):
                        ps = next_psA()

                        def t_s(t, r=r, a=a, b=b, qst=qst, kst=kst, ps=ps,
                                head_g=head_g, first_s=(first_s and ci == 0)):
                            if first_s:
                                t.wait_ge(sMR.h, head_g)
                                pguard(psPV, t)
                            pguard(ps, t)
                            eT.inc(t.matmul(ps[:, 0:b - a], kst[:, r * 128:(r + 1) * 128],
                                            qst[:, a:b], start=True, stop=True,
                                            skip_group_check=True))
                        sec("tensor", t_s)
                        tick = eT.n

                        def s_e(s, a=a, b=b, ps=ps, tick=tick, exp_ticks=exp_ticks):
                            s.wait_ge(eT.h, tick)
                            eS.inc(s.activation(p_sb[:, a:b], ps[:, 0:b - a], AF.Exp))
                            setg(ps, eS)
                            exp_ticks.append(eS.n)
                        sec("scalar", s_e)
                    first_s = False

                    def g_a(g, r=r, exp_ticks=exp_ticks):
                        g.wait_ge(eS.h, exp_ticks[0])
                        eG.inc(g.affine_select(p_sb[:, r * 128:(r + 1) * 128],
                                               p_sb[:, r * 128:(r + 1) * 128],
                                               pattern=[[1, 128]], compare_op=ALU.is_ge,
                                               fill=0.0, channel_multiplier=-1, base=0))
                    sec("gpsimd", g_a)
                    asel_tick = eG.n

                    if SKIP_PV_ODD and h % 2 == 1:
                        continue

                    def t_pv(t, r=r, h=h, chunks=chunks, exp_ticks=exp_ticks, asel_tick=asel_tick):
                        t.wait_ge(eS.h, exp_ticks[-1])
                        t.wait_ge(eG.h, asel_tick)
                        for a, b, c in chunks:
                            mm = t.matmul(psPV[0:65, a:b], v_sb[:, r, h * 65:h * 65 + 65],
                                          p_sb[:, a:b],
                                          start=(r == 0), stop=(r == min(15, 4 * c + 3)),
                                          skip_group_check=True)
                        eT.inc(mm)
                    sec("tensor", t_pv)
                pv_tick = eT.n
                if STOP == 430 + h:
                    def _finh(s):
                        s.wait_ge(dS.h, dS.n)
                        s.wait_ge(dG.h, dG.n)
                    sec("sync", _finh)
                    return nc
                if STOP <= 43:
                    def _fin43(s):
                        s.wait_ge(dS.h, dS.n)
                        s.wait_ge(dG.h, dG.n)
                    sec("sync", _fin43)
                    return nc


                for c in range(4):
                    def v_r(v, c=c, pv_tick=pv_tick):
                        v.wait_ge(eT.h, pv_tick)
                        v.tensor_copy(tmpf[0:1, 0:512], psPV[64:65, c * 512:(c + 1) * 512])
                        v.drain()
                        v.reciprocal(tmpf[0:1, 0:512], tmpf[0:1, 0:512])
                        v.drain()
                        eV.inc(v.nop())
                    sec("vector", v_r)
                    ps = next_psA()

                    def t_b(t, ps=ps):
                        t.wait_ge(eV.h, eV.n)
                        pguard(ps, t)
                        eT.inc(t.matmul(ps[0:64, 0:512], onesRF[0:1, 0:64], tmpf[0:1, 0:512],
                                        start=True, stop=True, skip_group_check=True))
                    sec("tensor", t_b)
                    bc_tick = eT.n

                    def v_m(v, c=c, h=h, ps=ps, bc_tick=bc_tick):
                        v.wait_ge(eT.h, bc_tick)
                        po = (h % 2) * 64
                        v.tensor_copy(p_sb[0:64, 0:512], ps[0:64, 0:512])
                        v.drain()
                        dst = acat[po:po + 64, h // 2, c * 512:(c + 1) * 512]
                        eV.inc(v.tensor_tensor(dst, psPV[0:64, c * 512:(c + 1) * 512],
                                               p_sb[0:64, 0:512], ALU.mult))
                        setg(ps, eV)
                    sec("vector", v_m)
                setg(psPV, eV)
                if STOP == 440 + h:
                    def _finh2(s):
                        s.wait_ge(dS.h, dS.n)
                        s.wait_ge(dG.h, dG.n)
                    sec("sync", _finh2)
                    return nc
                if STOP <= 44:
                    def _fin44(s):
                        s.wait_ge(dS.h, dS.n)
                        s.wait_ge(dG.h, dG.n)
                    sec("sync", _fin44)
                    return nc
            wqk_free[slp] = eT.n
            if STOP <= 45:
                def _fin45(s):
                    s.wait_ge(dS.h, dS.n)
                    s.wait_ge(dG.h, dG.n)
                sec("sync", _fin45)
                return nc


        xa_free = [None, None]
        if STOP <= 46:
            def _fin46(s):
                s.wait_ge(dS.h, dS.n)
                s.wait_ge(dG.h, dG.n)
            sec("sync", _fin46)
            return nc
        for mt in range(KT):
            for n4 in range(4):
                slx = (mt * 4 + n4) % 2

                def s_ld(s, mt=mt, n4=n4, slx=slx):
                    if xa_free[0] is not None:
                        s.wait_ge(eV.h, xa_free[0])
                    else:
                        s.wait_ge(eV.h, eV.n)
                    sXA.inc(s.dma_start(out=xin[:, 0, 0:512],
                                        in_=xa_d[mt * 128:(mt + 1) * 128, n4 * 512:(n4 + 1) * 512]), 16)
                sec("sync", s_ld)
                ld2 = sXA.n
                ps = next_psA()

                def t_fn(t, mt=mt, n4=n4, ps=ps):
                    pguard(ps, t)
                    for ek in range(KT):
                        mm = t.matmul(ps[:, 0:512], wo_sb[:, ek, mt * 128:(mt + 1) * 128],
                                      acat[:, ek, n4 * 512:(n4 + 1) * 512],
                                      start=(ek == 0), stop=(ek == KT - 1),
                                      skip_group_check=True)
                    eT.inc(mm)
                sec("tensor", t_fn)
                mm_tick = eT.n

                def v_fn(v, mt=mt, slx=slx, ps=ps, mm_tick=mm_tick, ld2=ld2):
                    v.wait_ge(eT.h, mm_tick)
                    v.wait_ge(sXA.h, ld2)
                    v.wait_ge(sX1.h, sX1.n)
                    last = v.scalar_tensor_tensor(x1st[:], ps[:, 0:512], ga_t[mt][:],
                                                  xin[:, 0, 0:512], ALU.mult, ALU.add)
                    v.drain()
                    eV.inc(last)
                    setg(ps, eV)
                sec("vector", v_fn)
                xa_free[0] = eV.n

                def s_st(s, mt=mt, n4=n4, tick=eV.n):
                    s.wait_ge(eV.h, tick)
                    sX1.inc(s.dma_start(out=x1_d[mt * 128:(mt + 1) * 128, n4 * 512:(n4 + 1) * 512],
                                        in_=x1st[:]), 16)
                sec("sync", s_st)
        x1_stored = sX1.n

        if STOP <= 5:
            def _fin5(s):
                s.wait_ge(dS.h, dS.n)
                s.wait_ge(dG.h, dG.n)
                dS.inc(s.dma_start(out=rowA[0:1, :], in_=x1_d[0:1, :]), 16)
                s.wait_ge(dS.h, dS.n)
                dO.inc(s.dma_start(out=dbg_h[1:2, :], in_=rowA[0:1, :]), 16)
                dO.inc(s.dma_start(out=dbg_h[5:6, 0:512], in_=x1st[0:1, :]), 16)
                s.wait_ge(dO.h, dO.n)

            def _fin5g(g):
                g.wait_ge(dO.h, dO.n)
                dO.inc(g.dma_start(out=dbg_h[0:1, 0:128], in_=lnw_t[0][:], allow_slow_non_contiguous=True), 16)
                dO.inc(g.dma_start(out=dbg_h[2:3, :], in_=y_sb[0:1, 0, :]), 16)
                dO.inc(g.dma_start(out=dbg_h[3:4, :], in_=acat[0:1, 0, :]), 16)
                dO.inc(g.dma_start(out=dbg_h[4:5, :], in_=qst_e[0:1, :]), 16)
                dO.inc(g.dma_start(out=dbg_h[6:7, 0:1040], in_=v_sb[0:1, 0, :]), 16)
                dO.inc(g.dma_start(out=dbg_h[7:8, 0:128], in_=xin[0:1, 0, 0:128]), 16)
                g.wait_ge(dO.h, dO.n)
            sec("sync", _fin5)
            sec("gpsimd", _fin5g)
            return nc

        x1_free = [None, None]
        for kt in range(KT):
            for half in range(2):
                slx = (kt * 2 + half) % 2

                def s_ld(s, kt=kt, half=half, slx=slx):
                    if x1_free[0] is not None:
                        s.wait_ge(eS.h, x1_free[0])
                    s.wait_ge(sX1.h, x1_stored)
                    sX1.inc(s.dma_start(out=xin[:, 0, :],
                                        in_=x1_d[kt * 128:(kt + 1) * 128, half * 1024:(half + 1) * 1024]), 16)
                sec("sync", s_ld)
                tick = sX1.n

                def s_fn(s, kt=kt, half=half, slx=slx, tick=tick):
                    s.wait_ge(sX1.h, tick)
                    s.activation(xb[:, kt, half * 1024:(half + 1) * 1024], xin[:, 0, :], AF.Square)
                    eS.inc(s.copy(x1b[:, kt, half * 1024:(half + 1) * 1024], xin[:, 0, :]))
                sec("scalar", s_fn)
                x1_free[0] = eS.n
        emit_ln_stats(x1b, xb)
        for n4 in range(4):
            psa, psb_, tick = emit_bcast(n4)

            def v_fn(v, psa=psa, psb_=psb_, tick=tick, n4=n4):
                v.wait_ge(eT.h, tick)
                for kt in range(KT):
                    sl = (slice(None), kt, slice(n4 * 512, (n4 + 1) * 512))
                    v.tensor_tensor(tmpf[:], x1b[sl], psa[:, 0:512], ALU.mult)
                    v.drain()
                    v.tensor_tensor(tmpf[:], tmpf[:], psb_[:, 0:512], ALU.add)
                    v.drain()
                    last = v.tensor_scalar(h2[sl], tmpf[:], cb_t[kt][:], sbv_t[kt][:], ALU.mult, ALU.add)
                    v.drain()
                v.drain()
                eV.inc(last)
                setg(psa, eV)
                setg(psb_, eV)
            sec("vector", v_fn)
        h2_tick = eV.n

        if STOP <= 6:
            def _fin5(s):
                s.wait_ge(dS.h, dS.n)
                s.wait_ge(dG.h, dG.n)
                dS.inc(s.dma_start(out=rowA[0:1, :], in_=x1_d[0:1, :]), 16)
                s.wait_ge(dS.h, dS.n)
                dO.inc(s.dma_start(out=dbg_h[1:2, :], in_=rowA[0:1, :]), 16)
                dO.inc(s.dma_start(out=dbg_h[5:6, 0:512], in_=x1st[0:1, :]), 16)
                s.wait_ge(dO.h, dO.n)

            def _fin5g(g):
                g.wait_ge(dO.h, dO.n)
                dO.inc(g.dma_start(out=dbg_h[0:1, 0:128], in_=lnw_t[0][:], allow_slow_non_contiguous=True), 16)
                dO.inc(g.dma_start(out=dbg_h[2:3, :], in_=y_sb[0:1, 0, :]), 16)
                dO.inc(g.dma_start(out=dbg_h[3:4, :], in_=acat[0:1, 0, :]), 16)
                dO.inc(g.dma_start(out=dbg_h[4:5, :], in_=qst_e[0:1, :]), 16)
                dO.inc(g.dma_start(out=dbg_h[6:7, 0:1040], in_=v_sb[0:1, 0, :]), 16)
                dO.inc(g.dma_start(out=dbg_h[7:8, 0:128], in_=xin[0:1, 0, 0:128]), 16)
                g.wait_ge(dO.h, dO.n)
            sec("sync", _fin5)
            sec("gpsimd", _fin5g)
            return nc

        def ph7_g(g):
            g.wait_ge(eV.h, h2_tick)
            for b8 in range(8):
                p2, hh = b8 // 4, b8 % 4
                srck = keys_h[hh:hh + 1, :, p2:p2 + 1, :].rearrange("a k b n -> (a b k) n")
                sKW.inc(g.dma_start(out=keysR[0:64, b8, :], in_=srck), 16)
                sKW.inc(g.dma_start(out=keysR[64:128, b8, :], in_=srck), 16)
            for kt in range(KT):
                sKW.inc(g.dma_start(out=wqr_sb[:, kt, :], in_=wqr_h[kt * 128:(kt + 1) * 128, :]), 16)
        sec("gpsimd", ph7_g)
        keys_tick = sKW.n

        for b8 in range(8):
            def t_k(t, b8=b8, keys_tick=keys_tick):
                if b8 == 0:
                    t.wait_ge(sKW.h, keys_tick)
                pguard(psB, t)
                t.transpose(psB[0:64, 0:64], keysR[0:64, b8, :], identB[0:64, 0:64])
                eT.inc(t.transpose(psB[64:128, 64:128], keysR[64:128, b8, :], identB[64:128, 64:128]))
            sec("tensor", t_k)
            tick = eT.n

            def v_k(v, b8=b8, tick=tick):
                v.wait_ge(eT.h, tick)
                v.tensor_copy(keysT[0:64, b8, :], psB[0:64, 0:64])
                last = v.tensor_copy(keysT[64:128, b8, :], psB[64:128, 64:128])
                v.drain()
                eV.inc(last)
                setg(psB, eV)
            sec("vector", v_k)

        de_free = [None, None]
        for et in range(32):
            sld = et % 2

            def g_ld(g, et=et, sld=sld):
                if de_free[sld] is not None:
                    g.wait_ge(eT.h, de_free[sld])
                sDE[sld].inc(g.dma_start(out=der[:, sld, :], in_=de_h[et * 128:(et + 1) * 128, :]), 16)
            sec("gpsimd", g_ld)
            tick = sDE[sld].n
            for kt in range(KT):
                def t_d(t, et=et, sld=sld, kt=kt, tick=tick):
                    if kt == 0:
                        t.wait_ge(sDE[sld].h, tick)
                    pguard(psB, t)
                    eT.inc(t.transpose(psB[:, 0:128], der[:, sld, kt * 128:(kt + 1) * 128], identB[:]))
                sec("tensor", t_d)
                tick2 = eT.n

                def v_d(v, et=et, kt=kt, tick2=tick2):
                    v.wait_ge(eT.h, tick2)
                    eV.inc(v.tensor_copy(det[:, kt, et * 128:(et + 1) * 128], psB[:, 0:128]))
                    setg(psB, eV)
                sec("vector", v_d)
            de_free[sld] = eT.n

        if STOP <= 7:
            def _fin7(s):
                s.wait_ge(dS.h, dS.n)
                s.wait_ge(dG.h, dG.n)
            sec("sync", _fin7)
            return nc
        ue_free = [None, None]
        for tc in range(4):
            for j in range(4):
                tt = 4 * tc + j
                for mt4 in range(4):
                    ps = next_psA()

                    def t_qr(t, tt=tt, mt4=mt4, ps=ps):
                        if mt4 == 0:
                            t.wait_ge(sKW.h, sKW.n)
                            t.wait_ge(eV.h, eV.n)
                        pguard(ps, t)
                        for kt in range(KT):
                            mm = t.matmul(ps[:, 0:128], wqr_sb[:, kt, mt4 * 128:(mt4 + 1) * 128],
                                          h2[:, kt, tt * 128:(tt + 1) * 128],
                                          start=(kt == 0), stop=(kt == KT - 1),
                                          skip_group_check=True)
                        eT.inc(mm)
                    sec("tensor", t_qr)
                    tick = eT.n

                    def s_qr(s, mt4=mt4, ps=ps, tick=tick):
                        if mt4 == 0:
                            s.wait_ge(sQ2.h, sQ2.n)
                        s.wait_ge(eT.h, tick)
                        eS.inc(s.copy(qrt[:, mt4, :], ps[:, 0:128]))
                        setg(ps, eS)
                    sec("scalar", s_qr)
                qr_tick = eS.n

                def s_sh(s, qr_tick=qr_tick):
                    s.wait_ge(eS.h, qr_tick)
                    sQ2.inc(s.dma_start(out=qrt2[0:64, :, :], in_=qrt[64:128, :, :]), 16)
                sec("sync", s_sh)
                sh_tick = sQ2.n
                ps_sim = next_psA()

                def t_sim(t, ps_sim=ps_sim, qr_tick=qr_tick, sh_tick=sh_tick):
                    t.wait_ge(eS.h, qr_tick)
                    t.wait_ge(sQ2.h, sh_tick)
                    pguard(ps_sim, t)
                    if SKIP_SIM:
                        eT.inc(t.engine_nop() if hasattr(t, "engine_nop") else t.nop())
                        return
                    for b8 in range(8):
                        p2, hh = b8 // 4, b8 % 4
                        rowbase = 256 * p2 + 64 * hh
                        mt4 = rowbase // 128
                        po = rowbase % 128
                        lhs = qrt[0:64, mt4, :] if po == 0 else qrt2[0:64, mt4, :]
                        mm = t.matmul(ps_sim[:, b8 * 64:(b8 + 1) * 64],
                                      lhs, keysT[0:64, b8, :],
                                      start=True, stop=True, skip_group_check=True)
                    eT.inc(mm)
                sec("tensor", t_sim)
                sim_tick = eT.n

                def s_sim(s, ps_sim=ps_sim, sim_tick=sim_tick):
                    s.wait_ge(eT.h, sim_tick)
                    eS.inc(s.copy(sim_f[:], ps_sim[:, 0:512]))
                    setg(ps_sim, eS)
                sec("scalar", s_sim)
                simf_tick = eS.n
                if STOP == 810:
                    def _fin810(s):
                        s.wait_ge(dS.h, dS.n)
                        s.wait_ge(dG.h, dG.n)
                    sec("sync", _fin810)
                    return nc

                def v_sm(v, simf_tick=simf_tick):
                    v.wait_ge(eS.h, simf_tick)
                    for hh in range(RH):
                        sx = sim_f[:, hh * 64:(hh + 1) * 64]
                        sy = sim_f[:, 256 + hh * 64:256 + (hh + 1) * 64]
                        v.tensor_reduce(c1_t[:], sx, AX.X, ALU.max)
                        v.tensor_reduce(c2_t[:], sy, AX.X, ALU.max)
                        v.drain()
                        v.nop()
                        v.drain()
                        v.max(out=t8[:], in_=sx)
                        v.drain()
                        v.match_replace(out=scr64[:], in_to_replace=t8[:], in_values=sx, imm_value=NEG)
                        v.drain()
                        v.tensor_reduce(s9x_t[:], scr64[:], AX.X, ALU.max)
                        v.max(out=t8y[:], in_=sy)
                        v.drain()
                        v.match_replace(out=scr64[:], in_to_replace=t8y[:], in_values=sy, imm_value=NEG)
                        v.drain()
                        v.tensor_reduce(s9y_t[:], scr64[:], AX.X, ALU.max)
                        v.drain()
                        v.memset(grid[:, 81:96], NEG)
                        g3 = grid[:, 0:64].rearrange("p (a b) -> p a b", b=8)
                        v.tensor_tensor(g3, t8[:].unsqueeze(2).broadcast_to([128, 8, 8]),
                                        t8y[:].unsqueeze(1).broadcast_to([128, 8, 8]), ALU.add)
                        v.tensor_scalar(grid[:, 64:72], t8[:], s9y_t[:], None, ALU.add)
                        v.tensor_scalar(grid[:, 72:80], t8y[:], s9x_t[:], None, ALU.add)
                        v.tensor_tensor(grid[:, 80:81], s9x_t[:], s9y_t[:], ALU.add)
                        v.drain()
                        v.nop()
                        v.drain()
                        v.max(out=m8g[:], in_=grid[:])
                        v.drain()
                        v.tensor_copy(th8_t[:], m8g[:, 7:8])
                        v.drain()
                        v.match_replace(out=g2[:], in_to_replace=m8g[:], in_values=grid[:], imm_value=NEG)
                        v.drain()
                        v.tensor_reduce(th9_t[:], g2[:], AX.X, ALU.max)
                        v.drain()
                        v.tensor_tensor(c12_t[:], c1_t[:], c2_t[:], ALU.add)
                        v.drain()
                        v.tensor_scalar(den8[:, hh * 8:(hh + 1) * 8], m8g[:], c12_t[:], None, ALU.subtract)
                        v.tensor_tensor(thm_t[:], th8_t[:], th9_t[:], ALU.add)
                        v.drain()
                        v.tensor_scalar(thm_t[:], thm_t[:], 0.5, None, ALU.mult)
                        v.drain()
                        v.tensor_scalar(r_f[:, hh, :], sy, thm_t[:], None, ALU.subtract)
                        v.drain()
                        v.tensor_scalar(r_f[:, hh, :], r_f[:, hh, :], -1.0, None, ALU.mult)
                        v.tensor_scalar(uvin[:, hh * 64:(hh + 1) * 64], sx, c1_t[:], None, ALU.subtract)
                        v.tensor_scalar(uvin[:, 256 + hh * 64:256 + (hh + 1) * 64], sy, c2_t[:], None, ALU.subtract)
                        v.drain()
                    v.drain()
                    eV.inc(v.nop())
                sec("vector", v_sm)
                if STOP == 811:
                    def _fin811(s):
                        s.wait_ge(dS.h, dS.n)
                        s.wait_ge(dG.h, dG.n)
                    sec("sync", _fin811)
                    return nc

                def s_ex(s):
                    s.wait_ge(eV.h, eV.n)
                    s.activation(uvin[:], uvin[:], AF.Exp)
                    eS.inc(s.activation(den8e[:], den8[:], AF.Exp))
                    s.wait_ge(eS.h, eS.n)
                sec("scalar", s_ex)

                def v_sm2(v, tick=eS.n):
                    v.wait_ge(eS.h, tick)
                    for hh in range(RH):
                        v.tensor_reduce(c1_t[:], den8e[:, hh * 8:(hh + 1) * 8], AX.X, ALU.add)
                        v.drain()
                        v.reciprocal(kap_t[:], c1_t[:])
                        v.drain()
                        v.tensor_scalar(u_p[:, hh, :], uvin[:, hh * 64:(hh + 1) * 64], kap_t[:], None, ALU.mult)
                        v.tensor_copy(v_p[:, hh, :], uvin[:, 256 + hh * 64:256 + (hh + 1) * 64])
                        v.drain()
                    for hh in range(RH):
                        sx = sim_f[:, hh * 64:(hh + 1) * 64]
                        m3 = mbuf[:].rearrange("p (a b) -> p a b", b=64)
                        v.tensor_tensor(m3, sx.unsqueeze(2).broadcast_to([128, 64, 64]),
                                        r_f[:, hh, :].unsqueeze(1).broadcast_to([128, 64, 64]), ALU.is_ge)
                        z3 = zbuf[:].rearrange("p (a b) -> p a b", b=64)
                        v.tensor_tensor(z3, u_p[:, hh, :].unsqueeze(2).broadcast_to([128, 64, 64]),
                                        v_p[:, hh, :].unsqueeze(1).broadcast_to([128, 64, 64]), ALU.mult)
                        v.drain()
                        if hh == 0:
                            v.tensor_tensor(abuf[:], mbuf[:], zbuf[:], ALU.mult)
                        else:
                            v.tensor_tensor(zbuf[:], mbuf[:], zbuf[:], ALU.mult)
                            v.drain()
                            v.tensor_tensor(abuf[:], abuf[:], zbuf[:], ALU.add)
                        v.drain()
                    v.drain()
                    eV.inc(v.nop())
                sec("vector", v_sm2)
                a_tick = eV.n
                if STOP == 812:
                    def _fin812(s):
                        s.wait_ge(dS.h, dS.n)
                        s.wait_ge(dG.h, dG.n)
                    sec("sync", _fin812)
                    return nc

                for wave in range(2):
                    def t_w(t, tt=tt, wave=wave):
                        pguard(psPV, t)
                        for kt in range(KT):
                            for ec in range(4):
                                e0 = (wave * 4 + ec) * 512
                                mm = t.matmul(psPV[:, ec * 512:(ec + 1) * 512],
                                              h2[:, kt, tt * 128:(tt + 1) * 128],
                                              det[:, kt, e0:e0 + 512],
                                              start=(kt == 0), stop=(kt == KT - 1),
                                              skip_group_check=True)
                        eT.inc(mm)
                    sec("tensor", t_w)
                    w_tick = eT.n

                    def s_w(s, wave=wave, w_tick=w_tick):
                        s.wait_ge(eT.h, w_tick)
                        for ec in range(4):
                            e0 = (wave * 4 + ec) * 512
                            eS.inc(s.activation(vbuf[:, e0:e0 + 512],
                                                psPV[:, ec * 512:(ec + 1) * 512], AF.Silu))
                    sec("scalar", s_w)
                    setg(psPV, eS)

                def v_w(v, a_tick=a_tick, tick=eS.n):
                    v.wait_ge(eS.h, tick)
                    v.wait_ge(eV.h, a_tick)
                    last = v.tensor_tensor(vbuf[:], vbuf[:], abuf[:], ALU.mult)
                    v.drain()
                    eV.inc(last)
                sec("vector", v_w)
                vb_tick = eV.n
                if STOP == 814:
                    def _fin814(s):
                        s.wait_ge(dS.h, dS.n)
                        s.wait_ge(dG.h, dG.n)
                    sec("sync", _fin814)
                    return nc

                for et in range(32):
                    def t_tr(t, et=et, vb_tick=vb_tick):
                        if et == 0:
                            t.wait_ge(eV.h, vb_tick)
                        pguard(psB, t)
                        eT.inc(t.transpose(psB[:, 0:128], vbuf[:, et * 128:(et + 1) * 128], identB[:]))
                    sec("tensor", t_tr)
                    tick = eT.n

                    def v_tr(v, et=et, j=j, tick=tick):
                        v.wait_ge(eT.h, tick)
                        eV.inc(v.tensor_copy(vt[:, et, j * 128:(j + 1) * 128], psB[:, 0:128]))
                        setg(psB, eV)
                    sec("vector", v_tr)

            vt_tick = eV.n
            if STOP <= 81:
                def _fin81(s):
                    s.wait_ge(dS.h, dS.n)
                    s.wait_ge(dG.h, dG.n)
                sec("sync", _fin81)
                return nc
            for grp in range(2):
                for et in range(32):
                    slu = et % 2

                    def s_ld(s, et=et, slu=slu, grp=grp):
                        if ue_free[slu] is not None:
                            s.wait_ge(eT.h, ue_free[slu])
                        if et == 0 and grp == 0:
                            s.wait_ge(sGB.h, 32)
                        sUE[slu].inc(s.dma_start(out=uer[:, slu, :], in_=ue_bf[et * 128:(et + 1) * 128, :]), 16)
                    sec("sync", s_ld)
                    tick = sUE[slu].n

                    def t_es(t, et=et, slu=slu, grp=grp, tick=tick, vt_tick=vt_tick):
                        t.wait_ge(sUE[slu].h, tick)
                        if et == 0:
                            t.wait_ge(eV.h, vt_tick)
                            pguard(psPV, t)
                        for ms in range(4):
                            mm = t.matmul(psPV[:, ms * 512:(ms + 1) * 512],
                                          uer[:, slu, (grp * 4 + ms) * 128:(grp * 4 + ms + 1) * 128],
                                          vt[:, et, :],
                                          start=(et == 0), stop=(et == 31),
                                          skip_group_check=True)
                        eT.inc(mm)
                    sec("tensor", t_es)
                    ue_free[slu] = eT.n
                es_tick = eT.n
                for ms in range(4):
                    def v_es(v, ms=ms, es_tick=es_tick):
                        v.wait_ge(eT.h, es_tick)
                        v.wait_ge(sES.h, sES.n)
                        last = v.tensor_copy(esr[:, ms % 2, :], psPV[:, ms * 512:(ms + 1) * 512])
                        v.drain()
                        eV.inc(last)
                    sec("vector", v_es)

                    def s_st(s, ms=ms, grp=grp, tc=tc, tick=eV.n):
                        s.wait_ge(eV.h, tick)
                        r0 = (grp * 4 + ms) * 128
                        sES.inc(s.dma_start(out=es_d[r0:r0 + 128, tc * 512:(tc + 1) * 512],
                                            in_=esr[:, ms % 2, :]), 16)
                    sec("sync", s_st)
                setg(psPV, eV)

        if STOP <= 88:
            def _fin88(s):
                s.wait_ge(dS.h, dS.n)
                s.wait_ge(dG.h, dG.n)
                dS.inc(s.dma_start(out=rowA[0:1, :], in_=x1_d[0:1, :]), 16)
                s.wait_ge(dS.h, dS.n)
                dO.inc(s.dma_start(out=dbg_h[1:2, :], in_=rowA[0:1, :]), 16)
                s.wait_ge(dO.h, dO.n)
            sec("sync", _fin88)
            return nc
        wg_free = [None, None]
        for kt in range(32):
            slw = kt % 2

            def g_ld(g, kt=kt, slw=slw):
                if kt == 0:
                    g.wait_ge(eT.h, eT.n)
                if wg_free[slw] is not None:
                    g.wait_ge(eT.h, wg_free[slw])
                srcg = wg_h[:, kt * 128:(kt + 1) * 128].rearrange("(dk p) m -> p dk m", p=128)
                srcu = wu_h[:, kt * 128:(kt + 1) * 128].rearrange("(dk p) m -> p dk m", p=128)
                sWG[slw].inc(g.dma_start(out=wgr[:, slw, :, :], in_=srcg), 16)
                sWG[slw].inc(g.dma_start(out=wur[:, slw, :, :], in_=srcu), 16)
            sec("gpsimd", g_ld)
            tick = sWG[slw].n
            for th in range(2):
                def t_gu(t, kt=kt, slw=slw, th=th, tick=tick):
                    t.wait_ge(sWG[slw].h, tick)
                    pguard(psPV, t)
                    for n5 in range(2):
                        for dk in range(KT):
                            mm = t.matmul(psPV[:, n5 * 512:(n5 + 1) * 512],
                                          wgr[:, slw, dk, :],
                                          h2[:, dk, th * 1024 + n5 * 512:th * 1024 + (n5 + 1) * 512],
                                          start=(dk == 0), stop=(dk == KT - 1),
                                          skip_group_check=True)
                        for dk in range(KT):
                            mm = t.matmul(psPV[:, 1024 + n5 * 512:1024 + (n5 + 1) * 512],
                                          wur[:, slw, dk, :],
                                          h2[:, dk, th * 1024 + n5 * 512:th * 1024 + (n5 + 1) * 512],
                                          start=(dk == 0), stop=(dk == KT - 1),
                                          skip_group_check=True)
                    eT.inc(mm)
                sec("tensor", t_gu)
                gu_tick = eT.n

                def s_gu(s, kt=kt, gu_tick=gu_tick):
                    s.wait_ge(eT.h, gu_tick)
                    eS.inc(s.activation(gstage[:], psPV[:, 0:1024], AF.Silu,
                                        bias=bg_t[kt][:], scale=1.0))
                sec("scalar", s_gu)

                def v_gu(v, kt=kt, th=th, tick=eS.n):
                    v.wait_ge(eS.h, tick)
                    v.wait_ge(sMC.h, sMC.n)
                    last = v.scalar_tensor_tensor(mring[:, kt % 2, th * 1024:(th + 1) * 1024],
                                                  psPV[:, 1024:2048], bu_t[kt][:], gstage[:],
                                                  ALU.add, ALU.mult)
                    v.drain()
                    eV.inc(last)
                sec("vector", v_gu)
                setg(psPV, eV)
            wg_free[slw] = eT.n

            def s_st(s, kt=kt, tick=eV.n):
                s.wait_ge(eV.h, tick)
                sMC.inc(s.dma_start(out=m_d[kt * 128:(kt + 1) * 128, :], in_=mring[:, kt % 2, :]), 16)
            sec("sync", s_st)

        if STOP <= 9:
            def _fin9(s):
                s.wait_ge(dS.h, dS.n)
                s.wait_ge(dG.h, dG.n)
                dS.inc(s.dma_start(out=rowA[0:1, :], in_=x1_d[0:1, :]), 16)
                s.wait_ge(dS.h, dS.n)
                dO.inc(s.dma_start(out=dbg_h[1:2, :], in_=rowA[0:1, :]), 16)
                s.wait_ge(dO.h, dO.n)
            sec("sync", _fin9)
            return nc
        for n4 in range(4):
            def s_mc(s, n4=n4):
                s.wait_ge(sMC.h, sMC.n)
                s.wait_ge(eT.h, eT.n)
                for kt in range(32):
                    sMC.inc(s.dma_start(out=mcache[:, kt, :],
                                        in_=m_d[kt * 128:(kt + 1) * 128, n4 * 512:(n4 + 1) * 512]), 16)
            sec("sync", s_mc)
            mc_tick = sMC.n
            for ms in range(KT):
                def g_wd(g, ms=ms, n4=n4):
                    g.wait_ge(eT.h, eT.n)
                    if ms == 0 and n4 == 0:
                        g.wait_ge(sGB.h, 32)
                    for kt in range(32):
                        sWD.inc(g.dma_start(out=wdt[:, kt, :],
                                            in_=wd_bf[kt * 128:(kt + 1) * 128, ms * 128:(ms + 1) * 128]), 16)
                sec("gpsimd", g_wd)
                wd_tick = sWD.n

                def s_ld2(s, ms=ms, n4=n4):
                    s.wait_ge(eV.h, eV.n)
                    sX1.inc(s.dma_start(out=xin[:, 0, 0:512],
                                        in_=x1_d[ms * 128:(ms + 1) * 128, n4 * 512:(n4 + 1) * 512]), 16)
                    sES.inc(s.dma_start(out=esr[:, 0, :],
                                        in_=es_d[ms * 128:(ms + 1) * 128, n4 * 512:(n4 + 1) * 512]), 16)
                sec("sync", s_ld2)
                ld3 = (sX1.n, sES.n)
                ps = next_psA()

                def t_y(t, ps=ps, wd_tick=wd_tick, mc_tick=mc_tick):
                    t.wait_ge(sWD.h, wd_tick)
                    t.wait_ge(sMC.h, mc_tick)
                    pguard(ps, t)
                    for kt in range(32):
                        mm = t.matmul(ps[:, 0:512], wdt[:, kt, :], mcache[:, kt, :],
                                      start=(kt == 0), stop=(kt == 31),
                                      skip_group_check=True)
                    eT.inc(mm)
                sec("tensor", t_y)
                y2 = eT.n

                def v_y(v, ms=ms, ps=ps, y2=y2, ld3=ld3):
                    v.wait_ge(eT.h, y2)
                    v.wait_ge(sX1.h, ld3[0])
                    v.wait_ge(sES.h, ld3[1])
                    v.tensor_scalar(tmpf[:], ps[:, 0:512], bd_t[ms][:], None, ALU.add)
                    v.drain()
                    v.tensor_tensor(tmpf[:], tmpf[:], esr[:, 0, :], ALU.add)
                    v.drain()
                    last = v.scalar_tensor_tensor(x1st[:], tmpf[:], gb_t[ms][:],
                                                  xin[:, 0, 0:512], ALU.mult, ALU.add)
                    v.drain()
                    eV.inc(last)
                    setg(ps, eV)
                sec("vector", v_y)
                cmb = eV.n

                for q in range(4):
                    ps2 = next_psA()

                    def t_tr(t, q=q, ps2=ps2, cmb=cmb):
                        t.wait_ge(eV.h, cmb)
                        pguard(ps2, t)
                        eT.inc(t.transpose(ps2[:, 0:128], x1st[:, q * 128:(q + 1) * 128], identF[:]))
                    sec("tensor", t_tr)
                    trt = eT.n

                    def s_ev(s, q=q, ps2=ps2, trt=trt):
                        s.wait_ge(eT.h, trt)
                        s.wait_ge(dO.h, dO.n)
                        eS.inc(s.copy(ostage[:, q * 128:(q + 1) * 128], ps2[:, 0:128]))
                        setg(ps2, eS)
                    sec("scalar", s_ev)

                    def s_st(s, ms=ms, n4=n4, q=q, tick=eS.n):
                        s.wait_ge(eS.h, tick)
                        dO.inc(s.dma_start(
                            out=out_h[n4 * 512 + q * 128:n4 * 512 + (q + 1) * 128,
                                      ms * 128:(ms + 1) * 128],
                            in_=ostage[:, q * 128:(q + 1) * 128]), 16)
                    sec("sync", s_st)

        import kernel as _K
        _K._DBG = {"h2": h2, "x1b": x1b, "sim_f": sim_f, "abuf": abuf, "vbuf": vbuf,
                   "vt": vt, "det": det, "qrt": qrt, "u_p": u_p, "v_p": v_p, "r_f": r_f,
                   "uvin": uvin, "den8e": den8e, "grid": grid, "mring": mring, "gstage": gstage,
                   "mcache": mcache, "wqr_sb": wqr_sb, "keysT": keysT}

        def fin(s):
            s.wait_ge(dO.h, dO.n)
            s.wait_ge(dS.h, dS.n)
            s.wait_ge(dG.h, dG.n)
            dS.inc(s.dma_start(out=rowA[0:1, :], in_=x1_d[0:1, :]), 16)
            s.wait_ge(dS.h, dS.n)
            dO.inc(s.dma_start(out=dbg_h[1:2, :], in_=rowA[0:1, :]), 16)
            s.wait_ge(dO.h, dO.n)
        sec("sync", fin)

    return nc


_CACHE = {}


def kernel(**inputs):
    if "nc" not in _CACHE:
        _CACHE["nc"] = build_nc()
    nc = _CACHE["nc"]
    B = 8
    x = np.ascontiguousarray(inputs["L0_attn_x"])
    c = np.ascontiguousarray(inputs["L0_c"])
    common = {
        "lnw": inputs["L0_attn_ln_w"], "lnb": inputs["L0_attn_ln_b"],
        "wq": inputs["L0_attn_wq"], "wk": inputs["L0_attn_wk"],
        "wv": inputs["L0_attn_wv"], "wo": inputs["L0_attn_wo"],
        "Wg": inputs["L0_ffn_Wg"], "bg": inputs["L0_ffn_bg"],
        "Wu": inputs["L0_ffn_Wu"], "bu": inputs["L0_ffn_bu"],
        "Wd": inputs["L0_ffn_Wd"], "bd": inputs["L0_ffn_bd"],
        "Wq": inputs["L0_ffn_Wq"], "keys": inputs["L0_ffn_keys"],
        "de": inputs["L0_ffn_down_embed"], "ue": inputs["L0_ffn_up_embed"],
        "wmod": inputs["L0_wmod"],
    }
    common = {k: np.ascontiguousarray(v, dtype=np.float32) for k, v in common.items()}
    in_maps = []
    for b in range(B):
        m = dict(common)
        m["x"] = np.ascontiguousarray(x[b])
        m["c"] = np.ascontiguousarray(c[b])
        in_maps.append(m)
    res = run_bass_kernel_spmd(nc, in_maps, core_ids=list(range(B)))
    out = np.stack([res.results[b]["out"] for b in range(B)], axis=0)
    return out.astype(np.float32)


# version 05 (final)
# speedup vs baseline: 1.0460x; 1.0460x over previous
import numpy as np
from contextlib import ExitStack
from functools import reduce

import concourse.bass as bass
import concourse.mybir as mybir
from concourse.bass_utils import run_bass_kernel_spmd

F32 = mybir.dt.float32
BF16 = mybir.dt.bfloat16
ALU = mybir.AluOpType
AF = mybir.ActivationFunctionType
AX = mybir.AxisListType

T = 2048
D = 1024
KT = 8
NH = 16
HD = 64
DCD = 4096
NE = 4096
RH = 4
EPS = 1e-5
NEG = -1e30


class Sem:
    def __init__(self, st, nc, name):
        self.h = st.enter_context(nc.semaphore(name))
        self.n = 0

    def inc(self, instr, k=1):
        instr.then_inc(self.h, k)
        self.n += k
        return instr


class Overlay:
    def __init__(self, nc, st):
        self.nc = nc
        self.st = st
        self.items = []
        self.size = 0

    def at(self, offset, name, shape, dtype):
        per_part = reduce(lambda a, b: a * b, shape[1:], 1) * mybir.dt.size(dtype)
        h = self.nc.virtual_sbuf_tensor(name, shape, dtype)
        self.items.append((h, offset))
        self.size = max(self.size, offset + per_part)
        return h

    def seal(self):
        sz = (self.size + 31) // 32 * 32
        self.base = self.st.enter_context(
            self.nc.sbuf_tensor("ovl_arena", [128, sz], mybir.dt.uint8))
        off0 = self.nc.lookup_mloc(self.base).addr
        for h, off in self.items:
            self.nc._tensor(h.name, h.shape, h.dtype, type="SB")
            mloc = self.nc.lookup_mloc(h)
            mloc.allocated = True
            mloc.addr = off0 + off


STOP = 99
SKIP_PV_ODD = False
SKIP_S_ODD = False
NORM_LVL = 4
SKIP_SIM = False


def build_nc():
    nc = bass.Bass("TRN2", target_bir_lowering=False)

    def din(name, shape):
        return nc.dram_tensor(name, list(shape), F32, kind="ExternalInput").ap()

    x_h = din("x", (T, D))
    lnw_h = din("lnw", (D,))
    lnb_h = din("lnb", (D,))
    wq_h = din("wq", (NH, D, HD))
    wk_h = din("wk", (NH, D, HD))
    wv_h = din("wv", (NH, D, HD))
    wo_h = din("wo", (D, D))
    wg_h = din("Wg", (D, DCD))
    bg_h = din("bg", (DCD,))
    wu_h = din("Wu", (D, DCD))
    bu_h = din("bu", (DCD,))
    wd_h = din("Wd", (DCD, D))
    bd_h = din("bd", (D,))
    wqr_h = din("Wq", (D, 512))
    keys_h = din("keys", (RH, 64, 2, 64))
    de_h = din("de", (NE, D))
    ue_h = din("ue", (NE, D))
    c_h = din("c", (D,))
    wmod_h = din("wmod", (D, 6 * D))

    out_h = nc.dram_tensor("out", [T, D], F32, kind="ExternalOutput").ap()
    dbg_h = nc.dram_tensor("dbg", [8, T], F32, kind="ExternalOutput").ap()
    xa_d = nc.dram_tensor("xa_d", [KT * 128, T], F32, kind="Internal").ap()
    x1_d = nc.dram_tensor("x1_d", [KT * 128, T], F32, kind="Internal").ap()
    m_d = nc.dram_tensor("m_d", [32 * 128, T], BF16, kind="Internal").ap()
    es_d = nc.dram_tensor("es_d", [KT * 128, T], BF16, kind="Internal").ap()
    ue_bf = nc.dram_tensor("ue_bf", [NE, D], BF16, kind="Internal").ap()
    wd_bf = nc.dram_tensor("wd_bf", [DCD, D], BF16, kind="Internal").ap()
    mod_d = nc.dram_tensor("mod_d", [6 * D], F32, kind="Internal").ap()

    st = ExitStack()
    with st:
        ovl = Overlay(nc, st)
        KB = 1024
        xT = ovl.at(0, "xT", [128, KT, T], F32)
        qst_e = ovl.at(0, "qst_e", [65, T], BF16)
        qst_o = ovl.at(4 * KB, "qst_o", [65, T], BF16)
        kst_e = ovl.at(8 * KB, "kst_e", [65, T], BF16)
        kst_o = ovl.at(12 * KB, "kst_o", [65, T], BF16)
        p_sb = ovl.at(16 * KB, "p_sb", [128, T], BF16)
        wqk = ovl.at(20 * KB, "wqk", [128, 2, KT, 256], BF16)
        wv_sb = ovl.at(28 * KB, "wv_sb", [128, KT, D], BF16)
        wo_sb = ovl.at(44 * KB, "wo_sb", [128, KT, D], BF16)
        qk_shift = ovl.at(60 * KB, "qk_shift", [128, T], BF16)
        det = ovl.at(0, "det", [128, KT, NE], BF16)
        A1 = 64 * KB
        h1b = ovl.at(A1, "h1b", [128, KT, T], BF16)
        onesF = ovl.at(A1, "onesF", [128, 128], F32)
        acat = ovl.at(A1, "acat", [128, KT, T], BF16)
        x1b = ovl.at(A1, "x1b", [128, KT, T], BF16)
        abuf = ovl.at(A1, "abuf", [128, NE], BF16)
        zbuf = ovl.at(A1 + 8 * KB, "zbuf", [128, NE], BF16)
        mbuf = ovl.at(A1 + 16 * KB, "mbuf", [128, NE], BF16)
        vbuf = ovl.at(A1 + 24 * KB, "vbuf", [128, NE], BF16)
        mcache = ovl.at(A1, "mcache", [128, 32, 512], BF16)
        A2 = 96 * KB
        xb = ovl.at(A2, "xb", [128, KT, T], BF16)
        y_sb = ovl.at(A2, "y_sb", [128, KT, T], BF16)
        h2 = ovl.at(A2, "h2", [128, KT, T], BF16)
        A3 = 128 * KB
        v_sb = ovl.at(A3, "v_sb", [128, 16, 1040], BF16)
        vt = ovl.at(A3, "vt", [128, 32, 512], BF16)
        A4 = 160 * KB + 512
        wmr = ovl.at(A4, "wmr", [128, 2, KT, 256], BF16)
        wgr = ovl.at(A4, "wgr", [128, 2, KT, 128], BF16)
        wur = ovl.at(A4 + 4 * KB, "wur", [128, 2, KT, 128], BF16)
        wdt = ovl.at(A4, "wdt", [128, 32, 128], BF16)
        der = ovl.at(A4, "der", [128, 2, D], BF16)
        uer = ovl.at(A4 + 4 * KB, "uer", [128, 2, D], BF16)
        A5 = A4 + 8 * KB
        qrt = ovl.at(A5, "qrt", [128, 4, 128], BF16)
        sq64 = ovl.at(A5 + 1 * KB, "sq64", [64, T], BF16)
        wqr_sb = ovl.at(A5 + 1 * KB, "wqr_sb", [128, KT, 512], BF16)
        mring = ovl.at(A5 + 1 * KB, "mring", [128, 2, T], BF16)
        xin = ovl.at(A3, "xin", [128, 1, D], F32)
        keysR = ovl.at(A3 + 28 * KB, "keysR", [128, 8, 64], BF16)
        sim_f = ovl.at(A4, "sim_f", [128, 512], F32)
        uvin = ovl.at(A4 + 2 * KB, "uvin", [128, 512], F32)
        A6 = A5 + 9 * KB
        x1st = ovl.at(A6, "x1st", [128, 512], F32)
        qrt2 = ovl.at(A6, "qrt2", [64, 4, 128], BF16)
        gstage = ovl.at(A6, "gstage", [128, 1024], BF16)
        ostage = ovl.at(A6 + 2 * KB, "ostage", [128, 512], F32)
        u_p = ovl.at(A6 + 2 * KB, "u_p", [128, RH, 64], BF16)
        v_p = ovl.at(A6 + 2 * KB + 512, "v_p", [128, RH, 64], BF16)
        r_f = ovl.at(A6 + 3 * KB, "r_f", [128, RH, 64], F32)
        grid = ovl.at(A6 + 4 * KB, "grid", [128, 96], F32)
        g2 = ovl.at(A6 + 4 * KB + 384, "g2", [128, 96], F32)
        scr64 = ovl.at(A6 + 4 * KB + 768, "scr64", [128, 64], F32)
        esr = ovl.at(A6 + 5 * KB + 64, "esr", [128, 2, 512], BF16)
        ovl.seal()

        def sbt(name, shape, dtype):
            return st.enter_context(nc.sbuf_tensor(name, shape, dtype))

        identF = sbt("identF", [128, 128], F32)
        identB = sbt("identB", [128, 128], BF16)
        onesb = sbt("onesb", [128, 1], BF16)
        onesRF = sbt("onesRF", [1, 128], F32)
        onesRB = sbt("onesRB", [1, 128], BF16)
        rowA = sbt("rowA", [1, T], F32)
        a_row = sbt("a_row", [1, T], BF16)
        kmax_t = sbt("kmax_t", [1, 1], F32)
        csil = sbt("csil", [128, 8], F32)
        csilb = sbt("csilb", [128, 8], BF16)
        tmpf = sbt("tmpf", [128, 512], F32)
        keysT = sbt("keysT", [128, 8, 64], BF16)
        m8g = sbt("m8g", [128, 8], F32)
        t8 = sbt("t8", [128, 8], F32)
        t8y = sbt("t8y", [128, 8], F32)
        den8 = sbt("den8", [128, RH * 8], F32)
        den8e = sbt("den8e", [128, RH * 8], F32)

        def cols8(tag):
            return [sbt(f"c_{tag}_{i}", [128, 1], F32) for i in range(8)]

        sa_t = cols8("sa")
        ca_t = cols8("ca")
        ga_t = cols8("ga")
        sbv_t = cols8("sb")
        cb_t = cols8("cb")
        gb_t = cols8("gb")
        lnw_t = cols8("lnw")
        lnb_t = cols8("lnb")
        bd_t = cols8("bd")
        bg_t = [sbt(f"bg_{i}", [128, 1], F32) for i in range(32)]
        bu_t = [sbt(f"bu_{i}", [128, 1], F32) for i in range(32)]
        c1_t = sbt("c1_t", [128, 1], F32)
        c2_t = sbt("c2_t", [128, 1], F32)
        c12_t = sbt("c12_t", [128, 1], F32)
        s9x_t = sbt("s9x_t", [128, 1], F32)
        s9y_t = sbt("s9y_t", [128, 1], F32)
        th8_t = sbt("th8_t", [128, 1], F32)
        th9_t = sbt("th9_t", [128, 1], F32)
        thm_t = sbt("thm_t", [128, 1], F32)
        kap_t = sbt("kap_t", [128, 1], F32)

        psA = [st.enter_context(nc.psum_tensor(f"psA{i}", [128, 512], F32))
               for i in range(3)]
        psPV = st.enter_context(nc.psum_tensor("psPV", [128, 2048], F32))
        psB = st.enter_context(nc.psum_tensor("psB", [128, 1024], BF16))

        dS = Sem(st, nc, "dS")
        sGB = Sem(st, nc, "sGB")
        sWM = [Sem(st, nc, "sWM0"), Sem(st, nc, "sWM1")]
        sWVO = Sem(st, nc, "sWVO")
        sQK = [Sem(st, nc, "sQK0"), Sem(st, nc, "sQK1")]
        sSH = Sem(st, nc, "sSH")
        sMR = Sem(st, nc, "sMR")
        sKW = Sem(st, nc, "sKW")
        sDE = [Sem(st, nc, "sDE0"), Sem(st, nc, "sDE1")]
        sWG = [Sem(st, nc, "sWG0"), Sem(st, nc, "sWG1")]
        sWD = Sem(st, nc, "sWD")
        sBI = Sem(st, nc, "sBI")
        sXIN = Sem(st, nc, "sXIN")
        sXA = Sem(st, nc, "sXA")
        sMOD = Sem(st, nc, "sMOD")
        sX1 = Sem(st, nc, "sX1")
        sMC = Sem(st, nc, "sMC")
        sUE = [Sem(st, nc, "sUE0"), Sem(st, nc, "sUE1")]
        sES = Sem(st, nc, "sES")
        sQ2 = Sem(st, nc, "sQ2")
        dG = Sem(st, nc, "dG")
        dO = Sem(st, nc, "dO")
        eT = Sem(st, nc, "eT")
        eV = Sem(st, nc, "eV")
        eS = Sem(st, nc, "eS")
        eG = Sem(st, nc, "eG")

        block = st.enter_context(nc.Block())
        guard = {}

        def pguard(t, eng):
            g = guard.get(id(t))
            if g:
                eng.wait_ge(g[0].h, g[1])

        def setg(t, sem):
            guard[id(t)] = (sem, sem.n)

        def sec(name, fn):
            getattr(block, name)(fn)

        rot = {"i": 0}

        def next_psA():
            rot["i"] = (rot["i"] + 1) % 3
            return psA[rot["i"]]

        def ph0_v(v):
            v.memset(onesF[:], 1.0)
            v.memset(onesb[:], 1.0)
            v.memset(onesRF[:], 1.0)
            v.memset(onesRB[:], 1.0)
            v.drain()
            eV.inc(v.nop())
        sec("vector", ph0_v)

        def ph0_g(g):
            g.wait_ge(eV.h, eV.n)
            eG.inc(g.affine_select(identF[:], onesF[:], pattern=[[1, 128]],
                                   compare_op=ALU.is_equal, fill=0.0,
                                   channel_multiplier=-1, base=0))
            sGB.inc(g.dma_start(out=ue_bf, in_=ue_h), 16)
            sGB.inc(g.dma_start(out=wd_bf, in_=wd_h), 16)
        sec("gpsimd", ph0_g)

        def ph0_v2(v):
            v.wait_ge(eG.h, eG.n)
            v.tensor_copy(identB[:], identF[:])
            v.drain()
            eV.inc(v.nop())
        sec("vector", ph0_v2)

        def ph0_s(s):
            for i in range(8):
                sBI.inc(s.dma_start(out=lnw_t[i][:], in_=lnw_h[i * 128:(i + 1) * 128].unsqueeze(1)), 16)
                sBI.inc(s.dma_start(out=lnb_t[i][:], in_=lnb_h[i * 128:(i + 1) * 128].unsqueeze(1)), 16)
                sBI.inc(s.dma_start(out=bd_t[i][:], in_=bd_h[i * 128:(i + 1) * 128].unsqueeze(1)), 16)
            for i in range(32):
                sBI.inc(s.dma_start(out=bg_t[i][:], in_=bg_h[i * 128:(i + 1) * 128].unsqueeze(1)), 16)
                sBI.inc(s.dma_start(out=bu_t[i][:], in_=bu_h[i * 128:(i + 1) * 128].unsqueeze(1)), 16)
            sBI.inc(s.dma_start(out=csil[:], in_=c_h.rearrange("(kt p) -> p kt", p=128), allow_slow_non_contiguous=True), 16)
        sec("sync", ph0_s)
        setup_dS = sBI.n

        xin_free = [None, None]
        for tt in range(16):
            sl = tt % 2

            def s_ld(s, tt=tt, sl=sl):
                if xin_free[0] is not None:
                    s.wait_ge(eT.h, xin_free[0])
                sXIN.inc(s.dma_start(out=xin[:, 0, :], in_=x_h[tt * 128:(tt + 1) * 128, :]), 16)
            sec("sync", s_ld)
            need = sXIN.n
            for kt in range(KT):
                ps = next_psA()

                def t_fn(t, tt=tt, kt=kt, need=need, ps=ps):
                    if kt == 0:
                        t.wait_ge(sXIN.h, need)
                    if tt == 0 and kt == 0:
                        t.wait_ge(eG.h, 1)
                    pguard(ps, t)
                    eT.inc(t.transpose(ps[:, 0:128], xin[:, 0, kt * 128:(kt + 1) * 128], identF[:]))
                sec("tensor", t_fn)
                tick = eT.n

                def s2_fn(s, tt=tt, kt=kt, ps=ps, tick=tick):
                    s.wait_ge(eT.h, tick)
                    eS.inc(s.copy(xT[:, kt, tt * 128:(tt + 1) * 128], ps[:, 0:128]))
                    setg(ps, eS)
                sec("scalar", s2_fn)
            xin_free[0] = eT.n

        if STOP <= 1:
            def _fin5(s):
                s.wait_ge(dS.h, dS.n)
                s.wait_ge(dG.h, dG.n)
                dS.inc(s.dma_start(out=rowA[0:1, :], in_=x1_d[0:1, :]), 16)
                s.wait_ge(dS.h, dS.n)
                dO.inc(s.dma_start(out=dbg_h[1:2, :], in_=rowA[0:1, :]), 16)
                dO.inc(s.dma_start(out=dbg_h[5:6, 0:512], in_=x1st[0:1, :]), 16)
                s.wait_ge(dO.h, dO.n)

            def _fin5g(g):
                g.wait_ge(dO.h, dO.n)
                dO.inc(g.dma_start(out=dbg_h[0:1, 0:128], in_=lnw_t[0][:], allow_slow_non_contiguous=True), 16)
                dO.inc(g.dma_start(out=dbg_h[2:3, :], in_=y_sb[0:1, 0, :]), 16)
                dO.inc(g.dma_start(out=dbg_h[3:4, :], in_=acat[0:1, 0, :]), 16)
                dO.inc(g.dma_start(out=dbg_h[4:5, :], in_=qst_e[0:1, :]), 16)
                dO.inc(g.dma_start(out=dbg_h[6:7, 0:1040], in_=v_sb[0:1, 0, :]), 16)
                dO.inc(g.dma_start(out=dbg_h[7:8, 0:128], in_=xin[0:1, 0, 0:128]), 16)
                g.wait_ge(dO.h, dO.n)
            sec("sync", _fin5)
            sec("gpsimd", _fin5g)
            return nc

        def ph2_s(s):
            s.wait_ge(sBI.h, setup_dS)
            eS.inc(s.activation(csil[:], csil[:], AF.Silu))
        sec("scalar", ph2_s)

        def ph2_v(v):
            v.wait_ge(eS.h, eS.n)
            v.drain()
            eV.inc(v.tensor_copy(csilb[:], csil[:]))
        sec("vector", ph2_v)
        csil_tick = eV.n

        wm_free = [None, None]
        for cc in range(24):
            sl = cc % 2

            def g_fn(g, cc=cc, sl=sl):
                if wm_free[sl] is not None:
                    g.wait_ge(eT.h, wm_free[sl])
                for kt in range(KT):
                    src = wmod_h[kt * 128:(kt + 1) * 128, cc * 256:(cc + 1) * 256]
                    sWM[sl].inc(g.dma_start(out=wmr[:, sl, kt, :], in_=src), 16)
            sec("gpsimd", g_fn)
            need = sWM[sl].n
            ps = next_psA()

            def t_fn(t, cc=cc, sl=sl, need=need, ps=ps):
                t.wait_ge(sWM[sl].h, need)
                if cc == 0:
                    t.wait_ge(eV.h, csil_tick)
                pguard(ps, t)
                for kt in range(KT):
                    mm = t.matmul(ps[0:1, 0:256], csilb[:, kt:kt + 1], wmr[:, sl, kt, :],
                                  start=(kt == 0), stop=(kt == KT - 1),
                                  skip_group_check=True)
                eT.inc(mm)
            sec("tensor", t_fn)
            wm_free[sl] = eT.n
            mm_tick = eT.n

            def s_fn(s, ps=ps, mm_tick=mm_tick):
                s.wait_ge(eT.h, mm_tick)
                s.wait_ge(sMOD.h, sMOD.n)
                eS.inc(s.copy(rowA[0:1, 0:256], ps[0:1, 0:256]))
                setg(ps, eS)
            sec("scalar", s_fn)

            def s_st(sy, cc=cc, tick=eS.n):
                sy.wait_ge(eS.h, tick)
                sMOD.inc(sy.dma_start(out=mod_d[cc * 256:(cc + 1) * 256].unsqueeze(0),
                                       in_=rowA[0:1, 0:256]), 16)
            sec("sync", s_st)

        def ph2_reload(s):
            s.wait_ge(sMOD.h, sMOD.n)
            tiles = [sa_t, ca_t, ga_t, sbv_t, cb_t, gb_t]
            for gset in range(6):
                for i in range(8):
                    off = gset * 1024 + i * 128
                    sMOD.inc(s.dma_start(out=tiles[gset][i][:], in_=mod_d[off:off + 128].unsqueeze(1)), 16)
        sec("sync", ph2_reload)

        def ph2_clamp(v):
            v.wait_ge(sMOD.h, sMOD.n)
            v.wait_ge(sBI.h, sBI.n)
            for i in range(8):
                v.tensor_scalar(ca_t[i][:], ca_t[i][:], 1.0, None, ALU.add)
                v.tensor_scalar(cb_t[i][:], cb_t[i][:], 1.0, None, ALU.add)
            v.drain()
            for tset in (ga_t, gb_t):
                for i in range(8):
                    g = tset[i]
                    v.tensor_scalar(c1_t[:], g[:], 0.0, None, ALU.is_ge)
                    v.drain()
                    v.tensor_scalar(c1_t[:], c1_t[:], 2.0, -1.0, ALU.mult, ALU.add)
                    v.tensor_scalar(c2_t[:], g[:], -1.0, None, ALU.mult)
                    v.drain()
                    v.tensor_tensor(c2_t[:], c2_t[:], g[:], ALU.max)
                    v.drain()
                    v.tensor_scalar(c2_t[:], c2_t[:], 0.25, None, ALU.max)
                    v.drain()
                    v.tensor_tensor(g[:], c1_t[:], c2_t[:], ALU.mult)
                    v.drain()
            v.drain()
            eV.inc(v.nop())
        sec("vector", ph2_clamp)

        if STOP <= 2:
            def _fin(s):
                s.wait_ge(dS.h, dS.n)
                s.wait_ge(dG.h, dG.n)
                dS.inc(s.dma_start(out=rowA[0:1, :], in_=mod_d[0:T].unsqueeze(0)), 16)
                s.wait_ge(dS.h, dS.n)
                dO.inc(s.dma_start(out=dbg_h[4:5, :], in_=rowA[0:1, :]), 16)
                dO.inc(s.dma_start(out=dbg_h[6:7, 0:128], in_=lnw_t[0][:], allow_slow_non_contiguous=True), 16)
                dO.inc(s.dma_start(out=dbg_h[7:8, 0:128], in_=ca_t[0][:], allow_slow_non_contiguous=True), 16)
                s.wait_ge(dO.h, dO.n)
            sec("sync", _fin)
            return nc

        def emit_ln_stats(val, sq):
            first = [True]
            for which, srcv in ((0, val), (1, sq)):
                for n4 in range(4):
                    ps = next_psA()

                    def t_fn(t, which=which, srcv=srcv, n4=n4, ps=ps):
                        if first[0]:
                            t.wait_ge(eS.h, eS.n)
                            t.wait_ge(eV.h, eV.n)
                            first[0] = False
                        pguard(ps, t)
                        for kt in range(KT):
                            mm = t.matmul(ps[0:1, 0:512], onesb[:],
                                          srcv[:, kt, n4 * 512:(n4 + 1) * 512],
                                          start=(kt == 0), stop=(kt == KT - 1),
                                          skip_group_check=True)
                        eT.inc(mm)
                    sec("tensor", t_fn)
                    tick = eT.n

                    def s_fn(s, which=which, n4=n4, ps=ps, tick=tick):
                        if which == 0 and n4 == 0:
                            s.wait_ge(sMR.h, sMR.n)
                            s.wait_ge(sMOD.h, sMOD.n)
                        s.wait_ge(eT.h, tick)
                        dst = a_row[0:1, n4 * 512:n4 * 512 + 512] if which == 0 else rowA[0:1, n4 * 512:n4 * 512 + 512]
                        eS.inc(s.copy(dst, ps[0:1, 0:512]))
                        setg(ps, eS)
                    sec("scalar", s_fn)

            def v_fn(v):
                v.wait_ge(eS.h, eS.n)
                sm = a_row[0:1, 0:T]
                sq_ = rowA[0:1, 0:T]
                v.tensor_scalar(sm, sm, 1.0 / D, None, ALU.mult)
                v.tensor_scalar(sq_, sq_, 1.0 / D, None, ALU.mult)
                v.drain()
                for _c in range(4):
                    v.tensor_tensor(tmpf[0:1, 0:512], sm[0:1, _c * 512:(_c + 1) * 512],
                                    sm[0:1, _c * 512:(_c + 1) * 512], ALU.mult)
                    v.drain()
                    v.tensor_tensor(rowA[0:1, _c * 512:(_c + 1) * 512],
                                    rowA[0:1, _c * 512:(_c + 1) * 512], tmpf[0:1, 0:512], ALU.subtract)
                    v.drain()
                v.tensor_scalar(rowA[0:1, 0:T], rowA[0:1, 0:T], EPS, None, ALU.add)
                v.drain()
                eV.inc(v.nop())
            sec("vector", v_fn)

            def s2_fn(s):
                s.wait_ge(eV.h, eV.n)
                s.wait_ge(eS.h, eS.n)
                eS.inc(s.activation(rowA[0:1, 0:T], rowA[0:1, 0:T], AF.Sqrt))
            sec("scalar", s2_fn)

            def v2_fn(v):
                v.wait_ge(eS.h, eS.n)
                v.reciprocal(rowA[0:1, 0:T], rowA[0:1, 0:T])
                v.drain()
                v.scalar_tensor_tensor(a_row[0:1, 0:T], a_row[0:1, 0:T], -1.0,
                                       rowA[0:1, 0:T], ALU.mult, ALU.mult)
                v.drain()
                eV.inc(v.nop())
            sec("vector", v2_fn)

        def emit_bcast(n4):
            psa, psb_ = next_psA(), next_psA()

            def t_fn(t, psa=psa, psb_=psb_, n4=n4):
                t.wait_ge(eV.h, eV.n)
                pguard(psa, t)
                pguard(psb_, t)
                t.matmul(psa[:, 0:512], onesRF[0:1, :], rowA[0:1, n4 * 512:(n4 + 1) * 512],
                         start=True, stop=True, skip_group_check=True)
                eT.inc(t.matmul(psb_[:, 0:512], onesRB[0:1, :], a_row[0:1, n4 * 512:(n4 + 1) * 512],
                                start=True, stop=True, skip_group_check=True))
            sec("tensor", t_fn)
            return psa, psb_, eT.n

        def ph3_sq(s):
            for kt in range(KT):
                eS.inc(s.activation(xb[:, kt, :], xT[:, kt, :], AF.Square))
        sec("scalar", ph3_sq)

        def ph3_cp(v):
            v.wait_ge(eS.h, eS.n)
            for kt in range(KT):
                eV.inc(v.tensor_copy(h1b[:, kt, :], xT[:, kt, :]))
        sec("vector", ph3_cp)

        emit_ln_stats(h1b, xb)
        for n4 in range(4):
            psa, psb_, tick = emit_bcast(n4)

            def v_fn(v, psa=psa, psb_=psb_, tick=tick, n4=n4):
                v.wait_ge(eT.h, tick)
                for kt in range(KT):
                    sl = (slice(None), kt, slice(n4 * 512, (n4 + 1) * 512))
                    v.tensor_tensor(tmpf[:], xT[sl], psa[:, 0:512], ALU.mult)
                    v.drain()
                    v.tensor_tensor(tmpf[:], tmpf[:], psb_[:, 0:512], ALU.add)
                    v.drain()
                    v.tensor_scalar(h1b[sl], tmpf[:], ca_t[kt][:], sa_t[kt][:], ALU.mult, ALU.add)
                    v.drain()
                    last = v.scalar_tensor_tensor(xT[sl], h1b[sl], ga_t[kt][:], xT[sl], ALU.mult, ALU.add)
                    v.drain()
                v.drain()
                eV.inc(last)
                setg(psa, eV)
                setg(psb_, eV)
            sec("vector", v_fn)

        def ph3_spill(s):
            s.wait_ge(eV.h, eV.n)
            for kt in range(KT):
                sXA.inc(s.dma_start(out=xa_d[kt * 128:(kt + 1) * 128, :], in_=xT[:, kt, :]), 16)
        sec("sync", ph3_spill)
        xa_spill = sXA.n

        if STOP <= 3:
            def _fin3(s):
                s.wait_ge(dS.h, dS.n)
                s.wait_ge(dG.h, dG.n)
                dS.inc(s.dma_start(out=rowA[0:1, :], in_=xa_d[0:1, :]), 16)
                s.wait_ge(dS.h, dS.n)
                dO.inc(s.dma_start(out=dbg_h[0:1, :], in_=rowA[0:1, :]), 16)
                s.wait_ge(dO.h, dO.n)
            sec("sync", _fin3)
            return nc

        def ph4_sq(s):
            s.wait_ge(eV.h, eV.n)
            for kt in range(KT):
                eS.inc(s.activation(xb[:, kt, :], h1b[:, kt, :], AF.Square))
        sec("scalar", ph4_sq)
        emit_ln_stats(h1b, xb)
        for n4 in range(4):
            psa, psb_, tick = emit_bcast(n4)

            def v_fn(v, psa=psa, psb_=psb_, tick=tick, n4=n4):
                v.wait_ge(eT.h, tick)
                for kt in range(KT):
                    sl = (slice(None), kt, slice(n4 * 512, (n4 + 1) * 512))
                    v.tensor_tensor(tmpf[:], h1b[sl], psa[:, 0:512], ALU.mult)
                    v.drain()
                    v.tensor_tensor(tmpf[:], tmpf[:], psb_[:, 0:512], ALU.add)
                    v.drain()
                    last = v.tensor_scalar(y_sb[sl], tmpf[:], lnw_t[kt][:], lnb_t[kt][:], ALU.mult, ALU.add)
                    v.drain()
                v.drain()
                eV.inc(last)
                setg(psa, eV)
                setg(psb_, eV)
            sec("vector", v_fn)
        y_tick = eV.n

        if STOP <= 4:
            def _fin5(s):
                s.wait_ge(dS.h, dS.n)
                s.wait_ge(dG.h, dG.n)
                dS.inc(s.dma_start(out=rowA[0:1, :], in_=x1_d[0:1, :]), 16)
                s.wait_ge(dS.h, dS.n)
                dO.inc(s.dma_start(out=dbg_h[1:2, :], in_=rowA[0:1, :]), 16)
                dO.inc(s.dma_start(out=dbg_h[5:6, 0:512], in_=x1st[0:1, :]), 16)
                s.wait_ge(dO.h, dO.n)

            def _fin5g(g):
                g.wait_ge(dO.h, dO.n)
                dO.inc(g.dma_start(out=dbg_h[0:1, 0:128], in_=lnw_t[0][:], allow_slow_non_contiguous=True), 16)
                dO.inc(g.dma_start(out=dbg_h[2:3, :], in_=y_sb[0:1, 0, :]), 16)
                dO.inc(g.dma_start(out=dbg_h[3:4, :], in_=acat[0:1, 0, :]), 16)
                dO.inc(g.dma_start(out=dbg_h[4:5, :], in_=qst_e[0:1, :]), 16)
                dO.inc(g.dma_start(out=dbg_h[6:7, 0:1040], in_=v_sb[0:1, 0, :]), 16)
                dO.inc(g.dma_start(out=dbg_h[7:8, 0:128], in_=xin[0:1, 0, 0:128]), 16)
                g.wait_ge(dO.h, dO.n)
            sec("sync", _fin5)
            sec("gpsimd", _fin5g)
            return nc

        def ph5_g(g):
            g.wait_ge(sXA.h, xa_spill)
            g.memset(kst_e[64:65, :], -1.0)
            eG.inc(g.memset(kst_o[64:65, :], -1.0))
            for kt in range(KT):
                src = wv_h[:, kt * 128:(kt + 1) * 128, :].transpose([1, 0, 2])
                sWVO.inc(g.dma_start(out=wv_sb[:, kt, :].rearrange("p (h e) -> p h e", e=64), in_=src), 16)
            for kt in range(KT):
                sWVO.inc(g.dma_start(out=wo_sb[:, kt, :], in_=wo_h[kt * 128:(kt + 1) * 128, :]), 16)
        sec("gpsimd", ph5_g)
        wv_tick = sWVO.n

        for tt in range(16):
            for half in range(2):
                ps = next_psA()

                def t_fn(t, tt=tt, half=half, ps=ps):
                    if tt == 0 and half == 0:
                        t.wait_ge(sWVO.h, wv_tick)
                        t.wait_ge(eV.h, y_tick)
                        t.wait_ge(sXA.h, xa_spill)
                    pguard(ps, t)
                    for kt in range(KT):
                        mm = t.matmul(ps[:, 0:512], y_sb[:, kt, tt * 128:(tt + 1) * 128],
                                      wv_sb[:, kt, half * 512:(half + 1) * 512],
                                      start=(kt == 0), stop=(kt == KT - 1),
                                      skip_group_check=True)
                    eT.inc(mm)
                sec("tensor", t_fn)
                tick = eT.n

                def v_fn(v, tt=tt, half=half, ps=ps, tick=tick):
                    v.wait_ge(eT.h, tick)
                    dst = v_sb[:, tt, half * 520:(half + 1) * 520].rearrange("p (h e) -> p h e", e=65)[:, :, 0:64]
                    eV.inc(v.tensor_copy(dst, ps[:, 0:512].rearrange("p (h e) -> p h e", e=64)))
                    setg(ps, eV)
                sec("vector", v_fn)

        def ph5_ones(g):
            g.wait_ge(eV.h, eV.n)
            for tt in range(16):
                eG.inc(g.memset(v_sb[:, tt, :].rearrange("p (h e) -> p h e", e=65)[:, :, 64:65], 1.0))
        sec("gpsimd", ph5_ones)

        if STOP <= 41:
            def _fin41(s):
                s.wait_ge(dS.h, dS.n)
                s.wait_ge(dG.h, dG.n)
            sec("sync", _fin41)
            return nc

        wqk_free = [None, None]
        for pi in range(8):
            slp = pi % 2

            def g_ld(g, pi=pi, slp=slp):
                if wqk_free[slp] is not None:
                    g.wait_ge(eT.h, wqk_free[slp])
                for kt in range(KT):
                    srcq = wq_h[2 * pi:2 * pi + 2, kt * 128:(kt + 1) * 128, :].transpose([1, 0, 2])
                    srck = wk_h[2 * pi:2 * pi + 2, kt * 128:(kt + 1) * 128, :].transpose([1, 0, 2])
                    sQK[slp].inc(g.dma_start(out=wqk[:, slp, kt, 0:128].rearrange("p (h e) -> p h e", e=64), in_=srcq), 16)
                    sQK[slp].inc(g.dma_start(out=wqk[:, slp, kt, 128:256].rearrange("p (h e) -> p h e", e=64), in_=srck), 16)
            sec("gpsimd", g_ld)
            wqk_tick = sQK[slp].n

            for which in ("q", "k"):
                base = 0 if which == "q" else 128
                scale = 0.125 if which == "q" else 1.0
                st_e = qst_e if which == "q" else kst_e
                st_o = qst_o if which == "q" else kst_o
                for n4 in range(4):
                    ps = next_psA()

                    def t_fn(t, base=base, wqk_tick=wqk_tick, slp=slp, n4=n4, ps=ps):
                        if n4 == 0:
                            t.wait_ge(sQK[slp].h, wqk_tick)
                        pguard(ps, t)
                        for kt in range(KT):
                            mm = t.matmul(ps[:, 0:512], wqk[:, slp, kt, base:base + 128],
                                          y_sb[:, kt, n4 * 512:(n4 + 1) * 512],
                                          start=(kt == 0), stop=(kt == KT - 1),
                                          skip_group_check=True)
                        eT.inc(mm)
                    sec("tensor", t_fn)
                    tick = eT.n

                    def s_fn(s, st_e=st_e, scale=scale, n4=n4, ps=ps, tick=tick):
                        if n4 == 0:
                            s.wait_ge(sSH.h, sSH.n)
                        s.wait_ge(eT.h, tick)
                        s.mul(st_e[0:64, n4 * 512:(n4 + 1) * 512], ps[0:64, 0:512], scale)
                        eS.inc(s.mul(qk_shift[64:128, n4 * 512:(n4 + 1) * 512], ps[64:128, 0:512], scale))
                        setg(ps, eS)
                    sec("scalar", s_fn)
                shift_need = eS.n

                def g_sh(g, st_o=st_o, shift_need=shift_need):
                    g.wait_ge(eS.h, shift_need)
                    sSH.inc(g.dma_start(out=st_o[0:64, :], in_=qk_shift[64:128, :]), 16)
                sec("gpsimd", g_sh)
            pair_g = sSH.n
            pair_s = eS.n
            if STOP <= 42:
                def _fin42(s):
                    s.wait_ge(dS.h, dS.n)
                    s.wait_ge(dG.h, dG.n)
                sec("sync", _fin42)
                return nc


            for hh in range(2):
                h = 2 * pi + hh
                qst = qst_e if hh == 0 else qst_o
                kst = kst_e if hh == 0 else kst_o

                LVL = NORM_LVL if (h % 2 == 1) else 4

                def s_n1(s, qst=qst, pair_g=pair_g, pair_s=pair_s):
                    s.wait_ge(sSH.h, pair_g)
                    s.wait_ge(eS.h, pair_s)
                    if LVL == 1 or LVL >= 4:
                        eS.inc(s.activation(sq64[0:64, :], qst[0:64, :], AF.Square))

                def v_n1(v, qst=qst, pair_g=pair_g):
                    if LVL in (11, 12, 13, 14):
                        v.wait_ge(dG.h, pair_g)
                        v.wait_ge(eS.h, pair_s)
                        srcq = qst_e if LVL == 12 else qst
                        dstq = p_sb if LVL == 13 else sq64
                        if LVL == 14:
                            v.tensor_copy(p_sb[0:64, 0:512], qst[0:64, 0:512])
                        else:
                            v.tensor_tensor(dstq[0:64, :], srcq[0:64, :], srcq[0:64, :], ALU.mult)
                        v.drain()
                        eV.inc(v.nop())
                sec("scalar", s_n1)
                sec("vector", v_n1)
                for n4 in range(4):
                    ps = next_psA()

                    def t_n1(t, n4=n4, ps=ps):
                        if n4 == 0:
                            t.wait_ge(eS.h, eS.n)
                        pguard(ps, t)
                        eT.inc(t.matmul(ps[0:1, 0:512], onesb[0:64, :],
                                        sq64[0:64, n4 * 512:(n4 + 1) * 512],
                                        start=True, stop=True, skip_group_check=True))
                    sec("tensor", t_n1)
                    tick = eT.n

                    def s_n2a(s, n4=n4, ps=ps, tick=tick):
                        if n4 == 0:
                            s.wait_ge(sMR.h, sMR.n)
                        s.wait_ge(eT.h, tick)
                        eS.inc(s.copy(rowA[0:1, n4 * 512:n4 * 512 + 512], ps[0:1, 0:512]))
                        setg(ps, eS)
                    sec("scalar", s_n2a)

                def s_n2(s, kst=kst):
                    s.wait_ge(eS.h, eS.n)
                    s.activation(rowA[0:1, 0:T], rowA[0:1, 0:T], AF.Sqrt)
                    eS.inc(s.activation(sq64[0:64, :], kst[0:64, :], AF.Square))
                sec("scalar", s_n2)
                for n4 in range(4):
                    ps = next_psA()

                    def t_n2(t, n4=n4, ps=ps):
                        if n4 == 0:
                            t.wait_ge(eS.h, eS.n)
                        pguard(ps, t)
                        eT.inc(t.matmul(ps[0:1, 0:512], onesb[0:64, :],
                                        sq64[0:64, n4 * 512:(n4 + 1) * 512],
                                        start=True, stop=True, skip_group_check=True))
                    sec("tensor", t_n2)
                    tick = eT.n

                    def s_n3a(s, n4=n4, ps=ps, tick=tick):
                        s.wait_ge(eT.h, tick)
                        eS.inc(s.copy(a_row[0:1, n4 * 512:n4 * 512 + 512], ps[0:1, 0:512]))
                        setg(ps, eS)
                    sec("scalar", s_n3a)

                def s_n3(s):
                    s.wait_ge(eS.h, eS.n)
                    eS.inc(s.activation(a_row[0:1, 0:T], a_row[0:1, 0:T], AF.Sqrt))
                sec("scalar", s_n3)

                def v_n(v):
                    v.wait_ge(eS.h, eS.n)
                    if LVL < 3:
                        v.drain()
                        eV.inc(v.nop())
                        return
                    v.tensor_reduce(kmax_t[:], a_row[0:1, 0:T], AX.X, ALU.max)
                    v.drain()
                    v.tensor_tensor(rowA[0:1, 0:T], rowA[0:1, 0:T],
                                    kmax_t[0:1, 0:1].broadcast_to([1, T]), ALU.mult)
                    v.drain()
                    eV.inc(v.nop())
                sec("vector", v_n)

                def g_m(g, qst=qst):
                    g.wait_ge(eV.h, eV.n)
                    if LVL >= 4:
                        sMR.inc(g.dma_start(out=qst[64:65, :], in_=rowA[0:1, 0:T]), 16)
                sec("gpsimd", g_m)
                head_g = sMR.n

                first_s = True
                rrange = [] if (SKIP_S_ODD and h % 2 == 1) else list(range(16))
                for r in rrange:
                    chunks = []
                    n0 = 128 * r
                    for c in range(r // 4, 4):
                        a = max(n0, 512 * c)
                        b = 512 * (c + 1)
                        chunks.append((a, b, c))
                    exp_ticks = []
                    for ci, (a, b, c) in enumerate(chunks):
                        ps = next_psA()

                        def t_s(t, r=r, a=a, b=b, qst=qst, kst=kst, ps=ps,
                                head_g=head_g, first_s=(first_s and ci == 0)):
                            if first_s:
                                t.wait_ge(sMR.h, head_g)
                                pguard(psPV, t)
                            pguard(ps, t)
                            eT.inc(t.matmul(ps[:, 0:b - a], kst[:, r * 128:(r + 1) * 128],
                                            qst[:, a:b], start=True, stop=True,
                                            skip_group_check=True))
                        sec("tensor", t_s)
                        tick = eT.n

                        def s_e(s, a=a, b=b, ps=ps, tick=tick, exp_ticks=exp_ticks):
                            s.wait_ge(eT.h, tick)
                            eS.inc(s.activation(p_sb[:, a:b], ps[:, 0:b - a], AF.Exp))
                            setg(ps, eS)
                            exp_ticks.append(eS.n)
                        sec("scalar", s_e)
                    first_s = False

                    def g_a(g, r=r, exp_ticks=exp_ticks):
                        g.wait_ge(eS.h, exp_ticks[0])
                        eG.inc(g.affine_select(p_sb[:, r * 128:(r + 1) * 128],
                                               p_sb[:, r * 128:(r + 1) * 128],
                                               pattern=[[1, 128]], compare_op=ALU.is_ge,
                                               fill=0.0, channel_multiplier=-1, base=0))
                    sec("gpsimd", g_a)
                    asel_tick = eG.n

                    if SKIP_PV_ODD and h % 2 == 1:
                        continue

                    def t_pv(t, r=r, h=h, chunks=chunks, exp_ticks=exp_ticks, asel_tick=asel_tick):
                        t.wait_ge(eS.h, exp_ticks[-1])
                        t.wait_ge(eG.h, asel_tick)
                        for a, b, c in chunks:
                            mm = t.matmul(psPV[0:65, a:b], v_sb[:, r, h * 65:h * 65 + 65],
                                          p_sb[:, a:b],
                                          start=(r == 0), stop=(r == min(15, 4 * c + 3)),
                                          skip_group_check=True)
                        eT.inc(mm)
                    sec("tensor", t_pv)
                pv_tick = eT.n
                if STOP == 430 + h:
                    def _finh(s):
                        s.wait_ge(dS.h, dS.n)
                        s.wait_ge(dG.h, dG.n)
                    sec("sync", _finh)
                    return nc
                if STOP <= 43:
                    def _fin43(s):
                        s.wait_ge(dS.h, dS.n)
                        s.wait_ge(dG.h, dG.n)
                    sec("sync", _fin43)
                    return nc


                for c in range(4):
                    def v_r(v, c=c, pv_tick=pv_tick):
                        v.wait_ge(eT.h, pv_tick)
                        v.tensor_copy(tmpf[0:1, 0:512], psPV[64:65, c * 512:(c + 1) * 512])
                        v.drain()
                        v.reciprocal(tmpf[0:1, 0:512], tmpf[0:1, 0:512])
                        v.drain()
                        eV.inc(v.nop())
                    sec("vector", v_r)
                    ps = next_psA()

                    def t_b(t, ps=ps):
                        t.wait_ge(eV.h, eV.n)
                        pguard(ps, t)
                        eT.inc(t.matmul(ps[0:64, 0:512], onesRF[0:1, 0:64], tmpf[0:1, 0:512],
                                        start=True, stop=True, skip_group_check=True))
                    sec("tensor", t_b)
                    bc_tick = eT.n

                    def v_m(v, c=c, h=h, ps=ps, bc_tick=bc_tick):
                        v.wait_ge(eT.h, bc_tick)
                        po = (h % 2) * 64
                        v.tensor_copy(p_sb[0:64, 0:512], ps[0:64, 0:512])
                        v.drain()
                        dst = acat[po:po + 64, h // 2, c * 512:(c + 1) * 512]
                        eV.inc(v.tensor_tensor(dst, psPV[0:64, c * 512:(c + 1) * 512],
                                               p_sb[0:64, 0:512], ALU.mult))
                        setg(ps, eV)
                    sec("vector", v_m)
                setg(psPV, eV)
                if STOP == 440 + h:
                    def _finh2(s):
                        s.wait_ge(dS.h, dS.n)
                        s.wait_ge(dG.h, dG.n)
                    sec("sync", _finh2)
                    return nc
                if STOP <= 44:
                    def _fin44(s):
                        s.wait_ge(dS.h, dS.n)
                        s.wait_ge(dG.h, dG.n)
                    sec("sync", _fin44)
                    return nc
            wqk_free[slp] = eT.n
            if STOP <= 45:
                def _fin45(s):
                    s.wait_ge(dS.h, dS.n)
                    s.wait_ge(dG.h, dG.n)
                sec("sync", _fin45)
                return nc


        xa_free = [None, None]
        if STOP <= 46:
            def _fin46(s):
                s.wait_ge(dS.h, dS.n)
                s.wait_ge(dG.h, dG.n)
            sec("sync", _fin46)
            return nc
        for mt in range(KT):
            for n4 in range(4):
                slx = (mt * 4 + n4) % 2

                def s_ld(s, mt=mt, n4=n4, slx=slx):
                    if xa_free[0] is not None:
                        s.wait_ge(eV.h, xa_free[0])
                    else:
                        s.wait_ge(eV.h, eV.n)
                    sXA.inc(s.dma_start(out=xin[:, 0, 0:512],
                                        in_=xa_d[mt * 128:(mt + 1) * 128, n4 * 512:(n4 + 1) * 512]), 16)
                sec("sync", s_ld)
                ld2 = sXA.n
                ps = next_psA()

                def t_fn(t, mt=mt, n4=n4, ps=ps):
                    pguard(ps, t)
                    for ek in range(KT):
                        mm = t.matmul(ps[:, 0:512], wo_sb[:, ek, mt * 128:(mt + 1) * 128],
                                      acat[:, ek, n4 * 512:(n4 + 1) * 512],
                                      start=(ek == 0), stop=(ek == KT - 1),
                                      skip_group_check=True)
                    eT.inc(mm)
                sec("tensor", t_fn)
                mm_tick = eT.n

                def v_fn(v, mt=mt, slx=slx, ps=ps, mm_tick=mm_tick, ld2=ld2):
                    v.wait_ge(eT.h, mm_tick)
                    v.wait_ge(sXA.h, ld2)
                    v.wait_ge(sX1.h, sX1.n)
                    last = v.scalar_tensor_tensor(x1st[:], ps[:, 0:512], ga_t[mt][:],
                                                  xin[:, 0, 0:512], ALU.mult, ALU.add)
                    v.drain()
                    eV.inc(last)
                    setg(ps, eV)
                sec("vector", v_fn)
                xa_free[0] = eV.n

                def s_st(s, mt=mt, n4=n4, tick=eV.n):
                    s.wait_ge(eV.h, tick)
                    sX1.inc(s.dma_start(out=x1_d[mt * 128:(mt + 1) * 128, n4 * 512:(n4 + 1) * 512],
                                        in_=x1st[:]), 16)
                sec("sync", s_st)
        x1_stored = sX1.n

        if STOP <= 5:
            def _fin5(s):
                s.wait_ge(dS.h, dS.n)
                s.wait_ge(dG.h, dG.n)
                dS.inc(s.dma_start(out=rowA[0:1, :], in_=x1_d[0:1, :]), 16)
                s.wait_ge(dS.h, dS.n)
                dO.inc(s.dma_start(out=dbg_h[1:2, :], in_=rowA[0:1, :]), 16)
                dO.inc(s.dma_start(out=dbg_h[5:6, 0:512], in_=x1st[0:1, :]), 16)
                s.wait_ge(dO.h, dO.n)

            def _fin5g(g):
                g.wait_ge(dO.h, dO.n)
                dO.inc(g.dma_start(out=dbg_h[0:1, 0:128], in_=lnw_t[0][:], allow_slow_non_contiguous=True), 16)
                dO.inc(g.dma_start(out=dbg_h[2:3, :], in_=y_sb[0:1, 0, :]), 16)
                dO.inc(g.dma_start(out=dbg_h[3:4, :], in_=acat[0:1, 0, :]), 16)
                dO.inc(g.dma_start(out=dbg_h[4:5, :], in_=qst_e[0:1, :]), 16)
                dO.inc(g.dma_start(out=dbg_h[6:7, 0:1040], in_=v_sb[0:1, 0, :]), 16)
                dO.inc(g.dma_start(out=dbg_h[7:8, 0:128], in_=xin[0:1, 0, 0:128]), 16)
                g.wait_ge(dO.h, dO.n)
            sec("sync", _fin5)
            sec("gpsimd", _fin5g)
            return nc

        x1_free = [None, None]
        for kt in range(KT):
            for half in range(2):
                slx = (kt * 2 + half) % 2

                def s_ld(s, kt=kt, half=half, slx=slx):
                    if x1_free[0] is not None:
                        s.wait_ge(eS.h, x1_free[0])
                    s.wait_ge(sX1.h, x1_stored)
                    sX1.inc(s.dma_start(out=xin[:, 0, :],
                                        in_=x1_d[kt * 128:(kt + 1) * 128, half * 1024:(half + 1) * 1024]), 16)
                sec("sync", s_ld)
                tick = sX1.n

                def s_fn(s, kt=kt, half=half, slx=slx, tick=tick):
                    s.wait_ge(sX1.h, tick)
                    s.activation(xb[:, kt, half * 1024:(half + 1) * 1024], xin[:, 0, :], AF.Square)
                    eS.inc(s.copy(x1b[:, kt, half * 1024:(half + 1) * 1024], xin[:, 0, :]))
                sec("scalar", s_fn)
                x1_free[0] = eS.n
        emit_ln_stats(x1b, xb)
        for n4 in range(4):
            psa, psb_, tick = emit_bcast(n4)

            def v_fn(v, psa=psa, psb_=psb_, tick=tick, n4=n4):
                v.wait_ge(eT.h, tick)
                for kt in range(KT):
                    sl = (slice(None), kt, slice(n4 * 512, (n4 + 1) * 512))
                    v.tensor_tensor(tmpf[:], x1b[sl], psa[:, 0:512], ALU.mult)
                    v.drain()
                    v.tensor_tensor(tmpf[:], tmpf[:], psb_[:, 0:512], ALU.add)
                    v.drain()
                    last = v.tensor_scalar(h2[sl], tmpf[:], cb_t[kt][:], sbv_t[kt][:], ALU.mult, ALU.add)
                    v.drain()
                v.drain()
                eV.inc(last)
                setg(psa, eV)
                setg(psb_, eV)
            sec("vector", v_fn)
        h2_tick = eV.n

        if STOP <= 6:
            def _fin5(s):
                s.wait_ge(dS.h, dS.n)
                s.wait_ge(dG.h, dG.n)
                dS.inc(s.dma_start(out=rowA[0:1, :], in_=x1_d[0:1, :]), 16)
                s.wait_ge(dS.h, dS.n)
                dO.inc(s.dma_start(out=dbg_h[1:2, :], in_=rowA[0:1, :]), 16)
                dO.inc(s.dma_start(out=dbg_h[5:6, 0:512], in_=x1st[0:1, :]), 16)
                s.wait_ge(dO.h, dO.n)

            def _fin5g(g):
                g.wait_ge(dO.h, dO.n)
                dO.inc(g.dma_start(out=dbg_h[0:1, 0:128], in_=lnw_t[0][:], allow_slow_non_contiguous=True), 16)
                dO.inc(g.dma_start(out=dbg_h[2:3, :], in_=y_sb[0:1, 0, :]), 16)
                dO.inc(g.dma_start(out=dbg_h[3:4, :], in_=acat[0:1, 0, :]), 16)
                dO.inc(g.dma_start(out=dbg_h[4:5, :], in_=qst_e[0:1, :]), 16)
                dO.inc(g.dma_start(out=dbg_h[6:7, 0:1040], in_=v_sb[0:1, 0, :]), 16)
                dO.inc(g.dma_start(out=dbg_h[7:8, 0:128], in_=xin[0:1, 0, 0:128]), 16)
                g.wait_ge(dO.h, dO.n)
            sec("sync", _fin5)
            sec("gpsimd", _fin5g)
            return nc

        def ph7_g(g):
            g.wait_ge(eV.h, h2_tick)
            for b8 in range(8):
                p2, hh = b8 // 4, b8 % 4
                srck = keys_h[hh:hh + 1, :, p2:p2 + 1, :].rearrange("a k b n -> (a b k) n")
                sKW.inc(g.dma_start(out=keysR[0:64, b8, :], in_=srck), 16)
                sKW.inc(g.dma_start(out=keysR[64:128, b8, :], in_=srck), 16)
            for kt in range(KT):
                sKW.inc(g.dma_start(out=wqr_sb[:, kt, :], in_=wqr_h[kt * 128:(kt + 1) * 128, :]), 16)
        sec("gpsimd", ph7_g)
        keys_tick = sKW.n

        for b8 in range(8):
            def t_k(t, b8=b8, keys_tick=keys_tick):
                if b8 == 0:
                    t.wait_ge(sKW.h, keys_tick)
                pguard(psB, t)
                t.transpose(psB[0:64, 0:64], keysR[0:64, b8, :], identB[0:64, 0:64])
                eT.inc(t.transpose(psB[64:128, 64:128], keysR[64:128, b8, :], identB[64:128, 64:128]))
            sec("tensor", t_k)
            tick = eT.n

            def v_k(v, b8=b8, tick=tick):
                v.wait_ge(eT.h, tick)
                v.tensor_copy(keysT[0:64, b8, :], psB[0:64, 0:64])
                last = v.tensor_copy(keysT[64:128, b8, :], psB[64:128, 64:128])
                v.drain()
                eV.inc(last)
                setg(psB, eV)
            sec("vector", v_k)

        de_free = [None, None]
        for et in range(32):
            sld = et % 2

            def g_ld(g, et=et, sld=sld):
                if de_free[sld] is not None:
                    g.wait_ge(eT.h, de_free[sld])
                sDE[sld].inc(g.dma_start(out=der[:, sld, :], in_=de_h[et * 128:(et + 1) * 128, :]), 16)
            sec("gpsimd", g_ld)
            tick = sDE[sld].n
            for kt in range(KT):
                def t_d(t, et=et, sld=sld, kt=kt, tick=tick):
                    if kt == 0:
                        t.wait_ge(sDE[sld].h, tick)
                    pguard(psB, t)
                    eT.inc(t.transpose(psB[:, 0:128], der[:, sld, kt * 128:(kt + 1) * 128], identB[:]))
                sec("tensor", t_d)
                tick2 = eT.n

                def v_d(v, et=et, kt=kt, tick2=tick2):
                    v.wait_ge(eT.h, tick2)
                    eV.inc(v.tensor_copy(det[:, kt, et * 128:(et + 1) * 128], psB[:, 0:128]))
                    setg(psB, eV)
                sec("vector", v_d)
            de_free[sld] = eT.n

        if STOP <= 7:
            def _fin7(s):
                s.wait_ge(dS.h, dS.n)
                s.wait_ge(dG.h, dG.n)
            sec("sync", _fin7)
            return nc
        ue_free = [None, None]
        for tc in range(4):
            for j in range(4):
                tt = 4 * tc + j
                for mt4 in range(4):
                    ps = next_psA()

                    def t_qr(t, tt=tt, mt4=mt4, ps=ps):
                        if mt4 == 0:
                            t.wait_ge(sKW.h, sKW.n)
                            t.wait_ge(eV.h, eV.n)
                        pguard(ps, t)
                        for kt in range(KT):
                            mm = t.matmul(ps[:, 0:128], wqr_sb[:, kt, mt4 * 128:(mt4 + 1) * 128],
                                          h2[:, kt, tt * 128:(tt + 1) * 128],
                                          start=(kt == 0), stop=(kt == KT - 1),
                                          skip_group_check=True)
                        eT.inc(mm)
                    sec("tensor", t_qr)
                    tick = eT.n

                    def s_qr(s, mt4=mt4, ps=ps, tick=tick):
                        if mt4 == 0:
                            s.wait_ge(sQ2.h, sQ2.n)
                        s.wait_ge(eT.h, tick)
                        eS.inc(s.copy(qrt[:, mt4, :], ps[:, 0:128]))
                        setg(ps, eS)
                    sec("scalar", s_qr)
                qr_tick = eS.n

                def s_sh(s, qr_tick=qr_tick):
                    s.wait_ge(eS.h, qr_tick)
                    sQ2.inc(s.dma_start(out=qrt2[0:64, :, :], in_=qrt[64:128, :, :]), 16)
                sec("sync", s_sh)
                sh_tick = sQ2.n
                ps_sim = next_psA()

                def t_sim(t, ps_sim=ps_sim, qr_tick=qr_tick, sh_tick=sh_tick):
                    t.wait_ge(eS.h, qr_tick)
                    t.wait_ge(sQ2.h, sh_tick)
                    pguard(ps_sim, t)
                    if SKIP_SIM:
                        eT.inc(t.engine_nop() if hasattr(t, "engine_nop") else t.nop())
                        return
                    for b8 in range(8):
                        p2, hh = b8 // 4, b8 % 4
                        rowbase = 256 * p2 + 64 * hh
                        mt4 = rowbase // 128
                        po = rowbase % 128
                        lhs = qrt[0:64, mt4, :] if po == 0 else qrt2[0:64, mt4, :]
                        mm = t.matmul(ps_sim[:, b8 * 64:(b8 + 1) * 64],
                                      lhs, keysT[0:64, b8, :],
                                      start=True, stop=True, skip_group_check=True)
                    eT.inc(mm)
                sec("tensor", t_sim)
                sim_tick = eT.n

                def s_sim(s, ps_sim=ps_sim, sim_tick=sim_tick):
                    s.wait_ge(eT.h, sim_tick)
                    eS.inc(s.copy(sim_f[:], ps_sim[:, 0:512]))
                    setg(ps_sim, eS)
                sec("scalar", s_sim)
                simf_tick = eS.n
                if STOP == 810:
                    def _fin810(s):
                        s.wait_ge(dS.h, dS.n)
                        s.wait_ge(dG.h, dG.n)
                    sec("sync", _fin810)
                    return nc

                def v_sm(v, simf_tick=simf_tick):
                    v.wait_ge(eS.h, simf_tick)
                    for hh in range(RH):
                        sx = sim_f[:, hh * 64:(hh + 1) * 64]
                        sy = sim_f[:, 256 + hh * 64:256 + (hh + 1) * 64]
                        v.tensor_reduce(c1_t[:], sx, AX.X, ALU.max)
                        v.tensor_reduce(c2_t[:], sy, AX.X, ALU.max)
                        v.drain()
                        v.nop()
                        v.drain()
                        v.max(out=t8[:], in_=sx)
                        v.drain()
                        v.match_replace(out=scr64[:], in_to_replace=t8[:], in_values=sx, imm_value=NEG)
                        v.drain()
                        v.tensor_reduce(s9x_t[:], scr64[:], AX.X, ALU.max)
                        v.max(out=t8y[:], in_=sy)
                        v.drain()
                        v.match_replace(out=scr64[:], in_to_replace=t8y[:], in_values=sy, imm_value=NEG)
                        v.drain()
                        v.tensor_reduce(s9y_t[:], scr64[:], AX.X, ALU.max)
                        v.drain()
                        v.memset(grid[:, 81:96], NEG)
                        g3 = grid[:, 0:64].rearrange("p (a b) -> p a b", b=8)
                        v.tensor_tensor(g3, t8[:].unsqueeze(2).broadcast_to([128, 8, 8]),
                                        t8y[:].unsqueeze(1).broadcast_to([128, 8, 8]), ALU.add)
                        v.tensor_scalar(grid[:, 64:72], t8[:], s9y_t[:], None, ALU.add)
                        v.tensor_scalar(grid[:, 72:80], t8y[:], s9x_t[:], None, ALU.add)
                        v.tensor_tensor(grid[:, 80:81], s9x_t[:], s9y_t[:], ALU.add)
                        v.drain()
                        v.nop()
                        v.drain()
                        v.max(out=m8g[:], in_=grid[:])
                        v.drain()
                        v.tensor_copy(th8_t[:], m8g[:, 7:8])
                        v.drain()
                        v.match_replace(out=g2[:], in_to_replace=m8g[:], in_values=grid[:], imm_value=NEG)
                        v.drain()
                        v.tensor_reduce(th9_t[:], g2[:], AX.X, ALU.max)
                        v.drain()
                        v.tensor_tensor(c12_t[:], c1_t[:], c2_t[:], ALU.add)
                        v.drain()
                        v.tensor_scalar(den8[:, hh * 8:(hh + 1) * 8], m8g[:], c12_t[:], None, ALU.subtract)
                        v.tensor_tensor(thm_t[:], th8_t[:], th9_t[:], ALU.add)
                        v.drain()
                        v.tensor_scalar(thm_t[:], thm_t[:], 0.5, None, ALU.mult)
                        v.drain()
                        v.tensor_scalar(r_f[:, hh, :], sy, thm_t[:], None, ALU.subtract)
                        v.drain()
                        v.tensor_scalar(r_f[:, hh, :], r_f[:, hh, :], -1.0, None, ALU.mult)
                        v.tensor_scalar(uvin[:, hh * 64:(hh + 1) * 64], sx, c1_t[:], None, ALU.subtract)
                        v.tensor_scalar(uvin[:, 256 + hh * 64:256 + (hh + 1) * 64], sy, c2_t[:], None, ALU.subtract)
                        v.drain()
                    v.drain()
                    eV.inc(v.nop())
                sec("vector", v_sm)
                if STOP == 811:
                    def _fin811(s):
                        s.wait_ge(dS.h, dS.n)
                        s.wait_ge(dG.h, dG.n)
                    sec("sync", _fin811)
                    return nc

                def s_ex(s):
                    s.wait_ge(eV.h, eV.n)
                    s.activation(uvin[:], uvin[:], AF.Exp)
                    eS.inc(s.activation(den8e[:], den8[:], AF.Exp))
                    s.wait_ge(eS.h, eS.n)
                sec("scalar", s_ex)

                def v_sm2(v, tick=eS.n):
                    v.wait_ge(eS.h, tick)
                    for hh in range(RH):
                        v.tensor_reduce(c1_t[:], den8e[:, hh * 8:(hh + 1) * 8], AX.X, ALU.add)
                        v.drain()
                        v.reciprocal(kap_t[:], c1_t[:])
                        v.drain()
                        v.tensor_scalar(u_p[:, hh, :], uvin[:, hh * 64:(hh + 1) * 64], kap_t[:], None, ALU.mult)
                        v.tensor_copy(v_p[:, hh, :], uvin[:, 256 + hh * 64:256 + (hh + 1) * 64])
                        v.drain()
                    for hh in range(RH):
                        sx = sim_f[:, hh * 64:(hh + 1) * 64]
                        m3 = mbuf[:].rearrange("p (a b) -> p a b", b=64)
                        v.tensor_tensor(m3, sx.unsqueeze(2).broadcast_to([128, 64, 64]),
                                        r_f[:, hh, :].unsqueeze(1).broadcast_to([128, 64, 64]), ALU.is_ge)
                        z3 = zbuf[:].rearrange("p (a b) -> p a b", b=64)
                        v.tensor_tensor(z3, u_p[:, hh, :].unsqueeze(2).broadcast_to([128, 64, 64]),
                                        v_p[:, hh, :].unsqueeze(1).broadcast_to([128, 64, 64]), ALU.mult)
                        v.drain()
                        if hh == 0:
                            v.tensor_tensor(abuf[:], mbuf[:], zbuf[:], ALU.mult)
                        else:
                            v.tensor_tensor(zbuf[:], mbuf[:], zbuf[:], ALU.mult)
                            v.drain()
                            v.tensor_tensor(abuf[:], abuf[:], zbuf[:], ALU.add)
                        v.drain()
                    v.drain()
                    eV.inc(v.nop())
                sec("vector", v_sm2)
                a_tick = eV.n
                if STOP == 812:
                    def _fin812(s):
                        s.wait_ge(dS.h, dS.n)
                        s.wait_ge(dG.h, dG.n)
                    sec("sync", _fin812)
                    return nc

                for wave in range(2):
                    def t_w(t, tt=tt, wave=wave):
                        pguard(psPV, t)
                        for kt in range(KT):
                            for ec in range(4):
                                e0 = (wave * 4 + ec) * 512
                                mm = t.matmul(psPV[:, ec * 512:(ec + 1) * 512],
                                              h2[:, kt, tt * 128:(tt + 1) * 128],
                                              det[:, kt, e0:e0 + 512],
                                              start=(kt == 0), stop=(kt == KT - 1),
                                              skip_group_check=True)
                        eT.inc(mm)
                    sec("tensor", t_w)
                    w_tick = eT.n

                    def s_w(s, wave=wave, w_tick=w_tick):
                        s.wait_ge(eT.h, w_tick)
                        for ec in range(4):
                            e0 = (wave * 4 + ec) * 512
                            eS.inc(s.activation(vbuf[:, e0:e0 + 512],
                                                psPV[:, ec * 512:(ec + 1) * 512], AF.Silu))
                    sec("scalar", s_w)
                    setg(psPV, eS)

                def v_w(v, a_tick=a_tick, tick=eS.n):
                    v.wait_ge(eS.h, tick)
                    v.wait_ge(eV.h, a_tick)
                    last = v.tensor_tensor(vbuf[:], vbuf[:], abuf[:], ALU.mult)
                    v.drain()
                    eV.inc(last)
                sec("vector", v_w)
                vb_tick = eV.n
                if STOP == 814:
                    def _fin814(s):
                        s.wait_ge(dS.h, dS.n)
                        s.wait_ge(dG.h, dG.n)
                    sec("sync", _fin814)
                    return nc

                for et in range(32):
                    def t_tr(t, et=et, vb_tick=vb_tick):
                        if et == 0:
                            t.wait_ge(eV.h, vb_tick)
                        pguard(psB, t)
                        eT.inc(t.transpose(psB[:, 0:128], vbuf[:, et * 128:(et + 1) * 128], identB[:]))
                    sec("tensor", t_tr)
                    tick = eT.n

                    def v_tr(v, et=et, j=j, tick=tick):
                        v.wait_ge(eT.h, tick)
                        eV.inc(v.tensor_copy(vt[:, et, j * 128:(j + 1) * 128], psB[:, 0:128]))
                        setg(psB, eV)
                    sec("vector", v_tr)

            vt_tick = eV.n
            if STOP <= 81:
                def _fin81(s):
                    s.wait_ge(dS.h, dS.n)
                    s.wait_ge(dG.h, dG.n)
                sec("sync", _fin81)
                return nc
            for grp in range(2):
                for et in range(32):
                    slu = et % 2

                    def s_ld(s, et=et, slu=slu, grp=grp):
                        if ue_free[slu] is not None:
                            s.wait_ge(eT.h, ue_free[slu])
                        if et == 0 and grp == 0:
                            s.wait_ge(sGB.h, 32)
                        sUE[slu].inc(s.dma_start(out=uer[:, slu, :], in_=ue_bf[et * 128:(et + 1) * 128, :]), 16)
                    sec("sync", s_ld)
                    tick = sUE[slu].n

                    def t_es(t, et=et, slu=slu, grp=grp, tick=tick, vt_tick=vt_tick):
                        t.wait_ge(sUE[slu].h, tick)
                        if et == 0:
                            t.wait_ge(eV.h, vt_tick)
                            pguard(psPV, t)
                        for ms in range(4):
                            mm = t.matmul(psPV[:, ms * 512:(ms + 1) * 512],
                                          uer[:, slu, (grp * 4 + ms) * 128:(grp * 4 + ms + 1) * 128],
                                          vt[:, et, :],
                                          start=(et == 0), stop=(et == 31),
                                          skip_group_check=True)
                        eT.inc(mm)
                    sec("tensor", t_es)
                    ue_free[slu] = eT.n
                es_tick = eT.n
                for ms in range(4):
                    def v_es(v, ms=ms, es_tick=es_tick):
                        v.wait_ge(eT.h, es_tick)
                        v.wait_ge(sES.h, sES.n)
                        last = v.tensor_copy(esr[:, ms % 2, :], psPV[:, ms * 512:(ms + 1) * 512])
                        v.drain()
                        eV.inc(last)
                    sec("vector", v_es)

                    def s_st(s, ms=ms, grp=grp, tc=tc, tick=eV.n):
                        s.wait_ge(eV.h, tick)
                        r0 = (grp * 4 + ms) * 128
                        sES.inc(s.dma_start(out=es_d[r0:r0 + 128, tc * 512:(tc + 1) * 512],
                                            in_=esr[:, ms % 2, :]), 16)
                    sec("sync", s_st)
                setg(psPV, eV)

        if STOP <= 88:
            def _fin88(s):
                s.wait_ge(dS.h, dS.n)
                s.wait_ge(dG.h, dG.n)
                dS.inc(s.dma_start(out=rowA[0:1, :], in_=x1_d[0:1, :]), 16)
                s.wait_ge(dS.h, dS.n)
                dO.inc(s.dma_start(out=dbg_h[1:2, :], in_=rowA[0:1, :]), 16)
                s.wait_ge(dO.h, dO.n)
            sec("sync", _fin88)
            return nc
        wg_free = [None, None]
        for kt in range(32):
            slw = kt % 2

            def g_ld(g, kt=kt, slw=slw):
                if kt == 0:
                    g.wait_ge(eT.h, eT.n)
                if wg_free[slw] is not None:
                    g.wait_ge(eT.h, wg_free[slw])
                srcg = wg_h[:, kt * 128:(kt + 1) * 128].rearrange("(dk p) m -> p dk m", p=128)
                srcu = wu_h[:, kt * 128:(kt + 1) * 128].rearrange("(dk p) m -> p dk m", p=128)
                sWG[slw].inc(g.dma_start(out=wgr[:, slw, :, :], in_=srcg), 16)
                sWG[slw].inc(g.dma_start(out=wur[:, slw, :, :], in_=srcu), 16)
            sec("gpsimd", g_ld)
            tick = sWG[slw].n
            for th in range(2):
                def t_gu(t, kt=kt, slw=slw, th=th, tick=tick):
                    t.wait_ge(sWG[slw].h, tick)
                    pguard(psPV, t)
                    for n5 in range(2):
                        for dk in range(KT):
                            mm = t.matmul(psPV[:, n5 * 512:(n5 + 1) * 512],
                                          wgr[:, slw, dk, :],
                                          h2[:, dk, th * 1024 + n5 * 512:th * 1024 + (n5 + 1) * 512],
                                          start=(dk == 0), stop=(dk == KT - 1),
                                          skip_group_check=True)
                        for dk in range(KT):
                            mm = t.matmul(psPV[:, 1024 + n5 * 512:1024 + (n5 + 1) * 512],
                                          wur[:, slw, dk, :],
                                          h2[:, dk, th * 1024 + n5 * 512:th * 1024 + (n5 + 1) * 512],
                                          start=(dk == 0), stop=(dk == KT - 1),
                                          skip_group_check=True)
                    eT.inc(mm)
                sec("tensor", t_gu)
                gu_tick = eT.n

                def s_gu(s, kt=kt, gu_tick=gu_tick):
                    s.wait_ge(eT.h, gu_tick)
                    eS.inc(s.activation(gstage[:], psPV[:, 0:1024], AF.Silu,
                                        bias=bg_t[kt][:], scale=1.0))
                sec("scalar", s_gu)

                def v_gu(v, kt=kt, th=th, tick=eS.n):
                    v.wait_ge(eS.h, tick)
                    v.wait_ge(sMC.h, sMC.n)
                    last = v.scalar_tensor_tensor(mring[:, kt % 2, th * 1024:(th + 1) * 1024],
                                                  psPV[:, 1024:2048], bu_t[kt][:], gstage[:],
                                                  ALU.add, ALU.mult)
                    v.drain()
                    eV.inc(last)
                sec("vector", v_gu)
                setg(psPV, eV)
            wg_free[slw] = eT.n

            def s_st(s, kt=kt, tick=eV.n):
                s.wait_ge(eV.h, tick)
                sMC.inc(s.dma_start(out=m_d[kt * 128:(kt + 1) * 128, :], in_=mring[:, kt % 2, :]), 16)
            sec("sync", s_st)

        if STOP <= 9:
            def _fin9(s):
                s.wait_ge(dS.h, dS.n)
                s.wait_ge(dG.h, dG.n)
                dS.inc(s.dma_start(out=rowA[0:1, :], in_=x1_d[0:1, :]), 16)
                s.wait_ge(dS.h, dS.n)
                dO.inc(s.dma_start(out=dbg_h[1:2, :], in_=rowA[0:1, :]), 16)
                s.wait_ge(dO.h, dO.n)
            sec("sync", _fin9)
            return nc
        for n4 in range(4):
            def s_mc(s, n4=n4):
                s.wait_ge(sMC.h, sMC.n)
                s.wait_ge(eT.h, eT.n)
                for gblk in range(4):
                    srcm = m_d[gblk * 1024:(gblk + 1) * 1024, n4 * 512:(n4 + 1) * 512].rearrange("(kt p) t -> p kt t", p=128)
                    sMC.inc(s.dma_start(out=mcache[:, gblk * 8:(gblk + 1) * 8, :], in_=srcm), 16)
            sec("sync", s_mc)
            mc_tick = sMC.n
            for ms in range(KT):
                def g_wd(g, ms=ms, n4=n4):
                    g.wait_ge(eT.h, eT.n)
                    if ms == 0 and n4 == 0:
                        g.wait_ge(sGB.h, 32)
                    for gblk in range(4):
                        srcw = wd_bf[gblk * 1024:(gblk + 1) * 1024, ms * 128:(ms + 1) * 128].rearrange("(kt p) m -> p kt m", p=128)
                        sWD.inc(g.dma_start(out=wdt[:, gblk * 8:(gblk + 1) * 8, :], in_=srcw), 16)
                sec("gpsimd", g_wd)
                wd_tick = sWD.n

                def s_ld2(s, ms=ms, n4=n4):
                    s.wait_ge(eV.h, eV.n)
                    sX1.inc(s.dma_start(out=xin[:, 0, 0:512],
                                        in_=x1_d[ms * 128:(ms + 1) * 128, n4 * 512:(n4 + 1) * 512]), 16)
                    sES.inc(s.dma_start(out=esr[:, 0, :],
                                        in_=es_d[ms * 128:(ms + 1) * 128, n4 * 512:(n4 + 1) * 512]), 16)
                sec("sync", s_ld2)
                ld3 = (sX1.n, sES.n)
                ps = next_psA()

                def t_y(t, ps=ps, wd_tick=wd_tick, mc_tick=mc_tick):
                    t.wait_ge(sWD.h, wd_tick)
                    t.wait_ge(sMC.h, mc_tick)
                    pguard(ps, t)
                    for kt in range(32):
                        mm = t.matmul(ps[:, 0:512], wdt[:, kt, :], mcache[:, kt, :],
                                      start=(kt == 0), stop=(kt == 31),
                                      skip_group_check=True)
                    eT.inc(mm)
                sec("tensor", t_y)
                y2 = eT.n

                def v_y(v, ms=ms, ps=ps, y2=y2, ld3=ld3):
                    v.wait_ge(eT.h, y2)
                    v.wait_ge(sX1.h, ld3[0])
                    v.wait_ge(sES.h, ld3[1])
                    v.tensor_scalar(tmpf[:], ps[:, 0:512], bd_t[ms][:], None, ALU.add)
                    v.drain()
                    v.tensor_tensor(tmpf[:], tmpf[:], esr[:, 0, :], ALU.add)
                    v.drain()
                    last = v.scalar_tensor_tensor(x1st[:], tmpf[:], gb_t[ms][:],
                                                  xin[:, 0, 0:512], ALU.mult, ALU.add)
                    v.drain()
                    eV.inc(last)
                    setg(ps, eV)
                sec("vector", v_y)
                cmb = eV.n

                for q in range(4):
                    ps2 = next_psA()

                    def t_tr(t, q=q, ps2=ps2, cmb=cmb):
                        t.wait_ge(eV.h, cmb)
                        pguard(ps2, t)
                        eT.inc(t.transpose(ps2[:, 0:128], x1st[:, q * 128:(q + 1) * 128], identF[:]))
                    sec("tensor", t_tr)
                    trt = eT.n

                    def s_ev(s, q=q, ps2=ps2, trt=trt):
                        s.wait_ge(eT.h, trt)
                        s.wait_ge(dO.h, dO.n)
                        eS.inc(s.copy(ostage[:, q * 128:(q + 1) * 128], ps2[:, 0:128]))
                        setg(ps2, eS)
                    sec("scalar", s_ev)

                    def s_st(s, ms=ms, n4=n4, q=q, tick=eS.n):
                        s.wait_ge(eS.h, tick)
                        dO.inc(s.dma_start(
                            out=out_h[n4 * 512 + q * 128:n4 * 512 + (q + 1) * 128,
                                      ms * 128:(ms + 1) * 128],
                            in_=ostage[:, q * 128:(q + 1) * 128]), 16)
                    sec("sync", s_st)

        import kernel as _K
        _K._DBG = {"h2": h2, "x1b": x1b, "sim_f": sim_f, "abuf": abuf, "vbuf": vbuf,
                   "vt": vt, "det": det, "qrt": qrt, "u_p": u_p, "v_p": v_p, "r_f": r_f,
                   "uvin": uvin, "den8e": den8e, "grid": grid, "mring": mring, "gstage": gstage,
                   "mcache": mcache, "wqr_sb": wqr_sb, "keysT": keysT}

        def fin(s):
            s.wait_ge(dO.h, dO.n)
            s.wait_ge(dS.h, dS.n)
            s.wait_ge(dG.h, dG.n)
            dS.inc(s.dma_start(out=rowA[0:1, :], in_=x1_d[0:1, :]), 16)
            s.wait_ge(dS.h, dS.n)
            dO.inc(s.dma_start(out=dbg_h[1:2, :], in_=rowA[0:1, :]), 16)
            s.wait_ge(dO.h, dO.n)
        sec("sync", fin)

    return nc


_CACHE = {}


def kernel(**inputs):
    if "nc" not in _CACHE:
        _CACHE["nc"] = build_nc()
    nc = _CACHE["nc"]
    B = 8
    x = np.ascontiguousarray(inputs["L0_attn_x"])
    c = np.ascontiguousarray(inputs["L0_c"])
    common = {
        "lnw": inputs["L0_attn_ln_w"], "lnb": inputs["L0_attn_ln_b"],
        "wq": inputs["L0_attn_wq"], "wk": inputs["L0_attn_wk"],
        "wv": inputs["L0_attn_wv"], "wo": inputs["L0_attn_wo"],
        "Wg": inputs["L0_ffn_Wg"], "bg": inputs["L0_ffn_bg"],
        "Wu": inputs["L0_ffn_Wu"], "bu": inputs["L0_ffn_bu"],
        "Wd": inputs["L0_ffn_Wd"], "bd": inputs["L0_ffn_bd"],
        "Wq": inputs["L0_ffn_Wq"], "keys": inputs["L0_ffn_keys"],
        "de": inputs["L0_ffn_down_embed"], "ue": inputs["L0_ffn_up_embed"],
        "wmod": inputs["L0_wmod"],
    }
    common = {k: np.ascontiguousarray(v, dtype=np.float32) for k, v in common.items()}
    in_maps = []
    for b in range(B):
        m = dict(common)
        m["x"] = np.ascontiguousarray(x[b])
        m["c"] = np.ascontiguousarray(c[b])
        in_maps.append(m)
    res = run_bass_kernel_spmd(nc, in_maps, core_ids=list(range(B)))
    out = np.stack([res.results[b]["out"] for b in range(B)], axis=0)
    return out.astype(np.float32)
